# version 09
# baseline (speedup 1.0000x reference)
import numpy as np
import ml_dtypes
from contextlib import ExitStack

import concourse.bass as bass
import concourse.mybir as mybir
from concourse.bass_utils import run_bass_kernel_spmd

F32 = mybir.dt.float32
BF16 = mybir.dt.bfloat16
I32 = mybir.dt.int32
AF = mybir.ActivationFunctionType
ALU = mybir.AluOpType
AX = mybir.AxisListType
BFNP = ml_dtypes.bfloat16

D_MODEL = 1024
HID = 4096
SEQ = 2048
NB = 16
NCORES = 8
BPC = NB // NCORES
RT = BPC * SEQ
NTCH = RT // 512
H = 512
CC = 256
NCH = SEQ // CC
EPS = 1e-5
INV2PI = float(1.0/(2*np.pi))
MAGIC = 12582912.0
TWO_PI = float(2*np.pi)

ENGINES = ("pe", "act", "dve", "pool", "sp")
_SEC = {"pe": "tensor", "act": "scalar", "dve": "vector", "pool": "gpsimd", "sp": "sync"}


class Op:
    __slots__ = ("idx", "eng", "fn", "deps", "inc_amt", "cum", "name", "semkey")

    def __init__(self, idx, eng, fn, deps, inc_amt, name):
        self.idx = idx; self.eng = eng; self.fn = fn
        self.deps = deps; self.inc_amt = inc_amt
        self.cum = None; self.name = name


class Sched:
    def __init__(self, nc):
        self.nc = nc
        self.ops = []
        self.per_eng = {e: [] for e in ENGINES}
        self.last_writer = {}
        self.readers = {}
        self.needed = set()

    ALIAS = {"tb1": "mtB", "tb2": "rtB", "tbm": "stB", "tbs1": "fsB"}

    def add(self, eng, fn, reads=(), writes=(), deps=(), dma=False, name=""):
        reads = [self.ALIAS.get(b, b) for b in reads]
        writes = [self.ALIAS.get(b, b) for b in writes]
        dep_set = set(deps)
        for b in reads:
            if b in self.last_writer:
                dep_set.add(self.last_writer[b])
        for b in writes:
            if b in self.last_writer:
                dep_set.add(self.last_writer[b])
            dep_set.update(self.readers.get(b, ()))
        dep_set.discard(len(self.ops))
        op = Op(len(self.ops), eng, fn, sorted(dep_set), 16 if dma else 1, name)
        self.ops.append(op)
        self.per_eng[eng].append(op)
        for b in writes:
            self.last_writer[b] = op.idx
            self.readers[b] = set()
        for b in reads:
            self.readers.setdefault(b, set()).add(op.idx)
        return op.idx

    def emit(self, es, block):
        NDSEM = 12
        for op in self.ops:
            if op.inc_amt == 16:
                self.needed.add(op.idx)
            for d in op.deps:
                dep = self.ops[d]
                if dep.eng != op.eng or dep.inc_amt == 16 or dep.eng in ("dve", "act", "pool"):
                    self.needed.add(d)
        dma_ord = {e: 0 for e in ENGINES}
        cum = {}
        prev_slot_op = {}
        for op in self.ops:
            if op.inc_amt == 16:
                slot = dma_ord[op.eng] % NDSEM
                dma_ord[op.eng] += 1
                key = ("D", op.eng, slot)
                op.semkey = key
                cum[key] = cum.get(key, 0) + 16
                op.cum = cum[key]
                pk = (op.eng, slot)
                if pk in prev_slot_op:
                    op.deps = sorted(set(op.deps) | {prev_slot_op[pk]})
                prev_slot_op[pk] = op.idx
            elif op.idx in self.needed:
                key = ("C", op.eng)
                op.semkey = key
                cum[key] = cum.get(key, 0) + 1
                op.cum = cum[key]
            else:
                op.semkey = None
        sems = {}
        for key in cum:
            name = "_".join(str(x) for x in key)
            sems[key] = es.enter_context(self.nc.semaphore(f"S{name}"))
        for eng in ENGINES:
            ops = self.per_eng[eng]
            if not ops:
                continue
            sec_dec = getattr(block, _SEC[eng])

            def emit_engine(sec, ops=ops, eng=eng):
                hwm = {}
                for op in ops:
                    for d in op.deps:
                        dep = self.ops[d]
                        if dep.semkey is None:
                            assert dep.eng == eng, f"dep {dep.name} of {op.name} lacks sem"
                            continue
                        if dep.eng == eng and dep.inc_amt != 16 and eng == "pe":
                            continue
                        if hwm.get(dep.semkey, 0) < dep.cum:
                            sec.wait_ge(sems[dep.semkey], dep.cum)
                            hwm[dep.semkey] = dep.cum
                    last = op.fn(sec)
                    if op.semkey is not None and (op.idx in self.needed):
                        assert last is not None, f"op {op.name} returned no instruction"
                        last.then_inc(sems[op.semkey], op.inc_amt)
            sec_dec(emit_engine)


_CONST_CACHE = {}

def host_constants():
    if _CONST_CACHE:
        return _CONST_CACHE
    d = np.arange(D_MODEL)
    ang = 2*np.pi*np.outer(d, d)/D_MODEL
    _CONST_CACHE["cmat"] = np.cos(ang).astype(BFNP)
    _CONST_CACHE["smat"] = np.sin(ang).astype(BFNP)
    F2 = 2*CC
    m = np.arange(CC); f = np.arange(CC+1)
    a2 = 2*np.pi*np.outer(m, f)/F2
    fwdc = np.zeros((CC, 384), np.float32); fwds = np.zeros((CC, 384), np.float32)
    fwdc[:, :CC+1] = np.cos(a2)
    fwds[:, :CC+1] = -np.sin(a2)
    _CONST_CACHE["fwdc"] = fwdc.astype(BFNP)
    _CONST_CACHE["fwds"] = fwds.astype(BFNP)
    cf = np.where((f == 0) | (f == CC), 1.0, 2.0)
    ai = 2*np.pi*np.outer(f, np.arange(CC))/F2
    invc = np.zeros((384, CC), np.float32); invs = np.zeros((384, CC), np.float32)
    invc[:CC+1, :] = cf[:, None]*np.cos(ai)/F2
    invs[:CC+1, :] = -cf[:, None]*np.sin(ai)/F2
    _CONST_CACHE["invc"] = invc.astype(BFNP)
    _CONST_CACHE["invs"] = invs.astype(BFNP)
    _CONST_CACHE["identb"] = np.eye(128, dtype=BFNP)
    _CONST_CACHE["identf"] = np.eye(128, dtype=np.float32)
    _CONST_CACHE["jrow"] = np.tile(np.arange(257, dtype=np.float32), (128, 1))
    return _CONST_CACHE


def build_program(debug=()):
    nc = bass.Bass("TRN2", target_bir_lowering=False)
    dram = {}
    def din(name, shape, dt=F32):
        dram[name] = nc.dram_tensor(name, shape, dt, kind="ExternalInput").ap()
    din("x0", [D_MODEL, RT])
    din("wxg", [D_MODEL, HID])
    din("wdin", [D_MODEL, H])
    din("wgate", [H, HID])
    din("wout", [HID, D_MODEL])
    din("wfr", [D_MODEL, D_MODEL]); din("wfi", [D_MODEL, D_MODEL])
    din("crt", [512, 512]); din("cit", [512, 512])
    din("lr", [128, 4]); din("li", [128, 4])
    din("dvec", [1, 512])
    din("n1w", [128, 8]); din("n1b", [128, 8])
    din("n2w", [1, 512]); din("n2b", [1, 512])
    din("fbias", [1, D_MODEL]); din("bout", [1, D_MODEL])
    din("cmat", [D_MODEL, D_MODEL], BF16); din("smat", [D_MODEL, D_MODEL], BF16)
    din("fwdc", [CC, 384], BF16); din("fwds", [CC, 384], BF16)
    din("invc", [384, CC], BF16); din("invs", [384, CC], BF16)
    din("identf", [128, 128])
    din("jrow", [128, 257])
    yout = nc.dram_tensor("y", [D_MODEL, RT], F32, kind="ExternalOutput").ap()
    scr = {}
    scr["u1"] = nc.dram_tensor("scr_u1", [D_MODEL, RT], BF16, kind="Internal").ap()
    scr["x1v"] = nc.dram_tensor("scr_x1v", [D_MODEL, RT], F32, kind="Internal").ap()
    scr["xln"] = nc.dram_tensor("scr_xln", [D_MODEL, RT], BF16, kind="Internal").ap()
    scr["x1raw"] = nc.dram_tensor("scr_x1raw", [D_MODEL, RT], F32, kind="Internal").ap()
    scr["wxgb"] = nc.dram_tensor("scr_wxgb", [D_MODEL, 4096], BF16, kind="Internal").ap()
    scr["wgtb"] = nc.dram_tensor("scr_wgtb", [512, 4096], BF16, kind="Internal").ap()
    scr["woub"] = nc.dram_tensor("scr_woub", [4096, D_MODEL], BF16, kind="Internal").ap()
    for nm in ("pt_hi", "pt_lo", "qt_hi", "qt_lo", "m_hi", "m_lo",
               "wrhi", "wrlo", "wihi", "wilo"):
        scr[nm] = nc.dram_tensor("scr_" + nm, [D_MODEL, D_MODEL], BF16, kind="Internal").ap()

    dbg = {}
    def dout(name, shape, dt):
        dbg[name] = nc.dram_tensor("dbg_" + name, shape, dt, kind="ExternalOutput").ap()
    shapes = {
        "mmat": ([D_MODEL, D_MODEL], BF16),
        "pthi": ([D_MODEL, D_MODEL], BF16),
        "ptlo": ([D_MODEL, D_MODEL], BF16),
        "qthi": ([D_MODEL, D_MODEL], BF16),
        "wrhi": ([D_MODEL, D_MODEL], BF16),
        "wrlo": ([D_MODEL, D_MODEL], BF16),
        "mlo": ([D_MODEL, D_MODEL], BF16),
        "x1v": ([D_MODEL, RT], F32),
        "x1raw": ([D_MODEL, RT], F32),
        "ffnps": ([128, 512], F32),
        "pqltap": ([128, 8*128], BF16),
        "bfbtap": ([128, 8*512], BF16),
        "u1": ([D_MODEL, RT], BF16),
        "xln": ([D_MODEL, RT], BF16),
        "xn": ([RT, H], BF16),
        "kp": ([CC, H], BF16),
        "kfcr": ([384, H], BF16),
        "kfci": ([384, H], BF16),
        "srb": ([4*128, 257], BF16),
        "sib": ([4*128, 257], BF16),
        "cctr": ([512, 512], BF16),
        "nccti": ([512, 512], BF16),
        "dfm": ([H, RT], BF16),
    }
    for name in debug:
        dout(name, *shapes[name])

    es = ExitStack()
    t = {}
    with es:
        def T(n, s, d):
            t[n] = es.enter_context(nc.sbuf_tensor(n, s, d))
        T("H1", [128, 16384], BF16)
        T("H2", [128, 16384], BF16)
        T("pqs", [128, 2, 512], BF16)
        T("pql", [128, 3, 8, 128], BF16)
        T("c_fwdc", [128, 2, 384], BF16)
        T("c_fwds", [128, 2, 384], BF16)
        T("c_invc", [128, 3, 256], BF16)
        T("c_invs", [128, 3, 256], BF16)
        T("c_identf", [128, 128], F32)
        T("c_jrow", [128, 257], F32)
        T("mtB", [128, 512], F32)
        T("rtB", [128, 512], F32)
        T("stB", [128, 512], F32)
        T("fsB", [128, 512], F32)
        T("srb", [128, 4, 257], BF16)
        T("sib", [128, 4, 257], BF16)
        T("etr", [128, 2, 4, 128], BF16)
        T("eti", [128, 2, 4, 128], BF16)
        T("cctr", [128, 4, 512], BF16)
        T("nccti", [128, 4, 512], BF16)
        T("kfcr", [128, 3, 512], BF16)
        T("kfci", [128, 3, 512], BF16)
        T("lamc", [128, 16], F32)
        T("mcol", [128, 16], F32)
        T("n2sc", [128, 16], F32)
        T("epsc", [128, 4], F32)
        T("n2wt", [128, 512], BF16)
        T("n2bt", [128, 512], BF16)
        T("n2wb", [1, 512], BF16)
        T("n2bb", [1, 512], BF16)
        T("onesrow", [1, 512], BF16)
        T("ones1", [1, 128], BF16)
        T("fbrow", [1, 1024], BF16)
        T("borow", [1, 1024], BF16)
        T("c_n1w", [128, 8], F32)
        T("c_n1b", [128, 8], F32)
        T("onesbf", [128, 128], BF16)
        T("c_wdin", [128, 8, 512], BF16)
        T("xcA", [128, 8, 512], F32)
        T("xcB", [128, 8, 512], F32)
        T("bfA", [128, 8, 512], BF16)
        T("bfB", [128, 8, 512], BF16)
        T("mt", [128, 512], F32)
        T("rt_", [128, 512], F32)
        T("st_", [128, 512], F32)
        T("fscr", [128, 512], F32)
        T("xfr", [128, 3, 512], BF16)
        T("yfi", [128, 3, 512], BF16)
        T("wxgA", [128, 8, 256], BF16)
        T("wxgB", [128, 8, 256], BF16)
        T("wgtA", [128, 4, 256], BF16)
        T("wgtB", [128, 4, 256], BF16)
        T("wouA", [128, 1, 1024], BF16)
        T("wouB", [128, 1, 1024], BF16)
        t["kp"] = t["pqs"]
        t["dbrow"] = None
        for i in range(8):
            t[f"pb{i}"] = es.enter_context(nc.psum_tensor(f"pb{i}", [128, 512], F32))

        v = {}
        v["cmat"] = t["H1"][:, 0:8192].rearrange("p (a b) -> p a b", a=8)
        v["smat"] = t["H1"][:, 8192:16384].rearrange("p (a b) -> p a b", a=8)
        v["xn"] = t["H1"][:].rearrange("p (a b) -> p a b", a=32)
        v["xgP"] = t["H1"][:].rearrange("p (a b) -> p a b", a=32)
        v["wrhi"] = t["H2"][:, 0:4096].rearrange("p (a b) -> p a b", a=4)
        v["wrlo"] = t["H2"][:, 4096:8192].rearrange("p (a b) -> p a b", a=4)
        v["wihi"] = t["H2"][:, 8192:12288].rearrange("p (a b) -> p a b", a=4)
        v["wilo"] = t["H2"][:, 12288:16384].rearrange("p (a b) -> p a b", a=4)
        v["dfm"] = t["H2"][:].rearrange("p (a b) -> p a b", a=4)
        v["psir"] = t["xcA"][:, 0:4, :]
        v["psii"] = t["xcA"][:, 4:8, :]
        v["str_"] = t["xcB"][:, 0:4, :]
        v["sti"] = t["xcB"][:, 4:8, :]
        v["strb"] = t["bfB"][:, 0:4, :]
        v["nstib"] = t["bfB"][:, 4:8, :]
        v["dtm"] = [t["mt"], t["rt_"]]
        v["phR"] = t["wxgA"][:].rearrange("p a b -> p (a b)")
        v["phI"] = t["wxgB"][:].rearrange("p a b -> p (a b)")
        v["xf1"] = t["wgtA"][:].rearrange("p a b -> p (a b)")
        t["tb1"] = t["mtB"][:, 0:257]
        t["tb2"] = t["rtB"][:, 0:257]
        t["tbm"] = t["stB"][:, 0:257]
        t["tbs1"] = t["fsB"][:, 0:257]

        for _e in ("dve", "act", "pool"):
            CHAIN[_e] = (es.enter_context(nc.semaphore(f"CH{_e}")), 0)
        sch = Sched(nc)
        block = es.enter_context(nc.Block())
        emit_all(sch, t, v, dram, scr, yout, dbg, set(debug))
        sch.emit(es, block)
    return nc


def sdma(sch, bufs, dst, src, name="ld"):
    if isinstance(bufs, str):
        bufs = [bufs]
    return sch.add("sp", lambda s, d=dst, r=src: s.dma_start(out=d, in_=r),
                   writes=bufs, dma=True, name=name)

def gdma(sch, bufs, dst, src, name="ldg"):
    if isinstance(bufs, str):
        bufs = [bufs]
    return sch.add("pool", lambda g, d=dst, r=src: g.dma_start(out=d, in_=r),
                   writes=bufs, dma=True, name=name)

def st_sdma(sch, reads, dst, src, name="st"):
    return sch.add("pool", lambda s, d=dst, r=src: s.dma_start(out=d, in_=r),
                   reads=reads, dma=True, name=name)

def r128(ap):
    return ap.rearrange("(a p) f -> p a f", p=128)


DBG_TAPS = {}
CHAIN = {}

def BAR(sec, eng, ins):
    sem, cnt = CHAIN[eng]
    cnt += 1
    CHAIN[eng] = (sem, cnt)
    ins.then_inc(sem, 1)
    sec.wait_ge(sem, cnt)

def emit_all(sch, t, v, dram, scr, yout, dbg, dbgset):
    DBG_TAPS.clear()
    DBG_TAPS.update(dbg)
    emit_p0_loads(sch, t, v, dram)
    conv_jobs = []

    def conv_w(nm, key, rows, cols, bufs, bnames, flat_cols):
        ch = []
        for r0 in range(0, rows, 128):
            for c0 in range(0, cols, flat_cols):
                ch.append((r0, c0))
        for ci, (r0, c0) in enumerate(ch):
            def job(ci=ci, r0=r0, c0=c0, nm=nm, key=key, bufs=bufs, bnames=bnames,
                    flat_cols=flat_cols):
                b = ci % 2
                ap = bufs[b][:, 0:flat_cols]
                gdma(sch, [bnames[b]], ap, dram[nm][r0:r0+128, c0:c0+flat_cols],
                     name=f"cwl_{nm}{ci}")
                op = st_sdma(sch, [bnames[b]], scr[key][r0:r0+128, c0:c0+flat_cols], ap,
                             name=f"cws_{nm}{ci}")
                sch.last_writer[f"cw_{nm}"] = op
            conv_jobs.append(job)
    wxgf = [t["wxgA"][:].rearrange("p a b -> p (a b)"), t["wxgB"][:].rearrange("p a b -> p (a b)")]
    wgtf = [t["wgtA"][:].rearrange("p a b -> p (a b)"), t["wgtB"][:].rearrange("p a b -> p (a b)")]
    conv_w("wxg", "wxgb", D_MODEL, 4096, wxgf, ["wxg0", "wxg1"], 2048)
    conv_w("wgate", "wgtb", 512, 4096, wgtf, ["wgt0", "wgt1"], 1024)
    conv_w("wout", "woub", 4096, D_MODEL, wgtf, ["wgt0", "wgt1"], 1024)
    emit_wprep(sch, t, v, dram, scr)
    mparts = list(mbuild_parts(sch, t, v, scr))
    per = (len(mparts) + NTCH - 1) // NTCH
    cper = (len(conv_jobs) + NTCH - 1) // NTCH
    for tc in range(NTCH):
        for _ in range(per):
            if mparts:
                mparts.pop(0)()
        emit_p1_pre(sch, t, v, dram, scr, tc)
        for _ in range(cper):
            if conv_jobs:
                conv_jobs.pop(0)()
    while mparts:
        mparts.pop(0)()
    while conv_jobs:
        conv_jobs.pop(0)()
    emit_tables(sch, t, v, dram)
    for tc in range(NTCH):
        emit_p1_post(sch, t, v, dram, scr, tc)
    for sq in range(BPC):
        emit_p2_seq(sch, t, v, sq)
    for tc in range(NTCH):
        emit_p3_chunk(sch, t, v, dram, scr, tc)
        emit_p4_chunk(sch, t, v, dram, scr, yout, tc)
    emit_debug(sch, t, v, scr, dbg, dbgset)
    sch.add("sp", lambda s: s.nop(), reads=[f"yout{c}" for c in range(NTCH)], name="terminal")


def emit_p0_loads(sch, t, v, dram):
    A = sch.add
    sdma(sch, "c_fwdc", t["c_fwdc"][:], r128(dram["fwdc"]))
    sdma(sch, "c_fwds", t["c_fwds"][:], r128(dram["fwds"]))
    sdma(sch, "c_invc", t["c_invc"][:], r128(dram["invc"]))
    sdma(sch, "c_invs", t["c_invs"][:], r128(dram["invs"]))
    sdma(sch, "c_identf", t["c_identf"][:], dram["identf"])
    sdma(sch, "c_jrow", t["c_jrow"][:], dram["jrow"])
    sdma(sch, "c_n1w", t["c_n1w"][:], dram["n1w"])
    sdma(sch, "c_n1b", t["c_n1b"][:], dram["n1b"])
    sdma(sch, ["cmat", "H1"], v["cmat"], r128(dram["cmat"]))
    sdma(sch, ["smat", "H1"], v["smat"], r128(dram["smat"]))
    sdma(sch, [f"md_lr{n}" for n in range(4)], t["mcol"][:, 0:4], dram["lr"], name="ld_lr")
    sdma(sch, [f"md_li{n}" for n in range(4)], t["mcol"][:, 4:8], dram["li"], name="ld_li")
    gdma(sch, "fbrow", t["fbrow"][:], dram["fbias"])
    gdma(sch, "borow", t["borow"][:], dram["bout"])
    gdma(sch, "cctr_raw", t["cctr"][:], r128(dram["crt"]))
    gdma(sch, "ccti_raw", t["nccti"][:], r128(dram["cit"]))
    gdma(sch, "c_wdin", t["c_wdin"][:], r128(dram["wdin"]))
    gdma(sch, "n2wb", t["n2wb"][:], dram["n2w"])
    gdma(sch, "n2bb", t["n2bb"][:], dram["n2b"])
    gdma(sch, "dbrow", t["wouA"][0:1, 0, 0:512], dram["dvec"])
    def ms(buf, ap, val):
        A("dve", lambda vE, a=ap, q=val: vE.memset(a, q), writes=[buf], name=f"ms_{buf}")
    ms("onesbf", t["onesbf"][:], 1.0/1024.)
    ms("onesrow", t["onesrow"][:], 1.0)
    ms("ones1", t["ones1"][:], 1.0)
    A("dve", lambda vE: vE.memset(t["epsc"][:, 0:1], EPS), writes=["epsc0"], name="ms_eps")
    A("dve", lambda vE: vE.memset(t["epsc"][:, 1:2], EPS*EPS), writes=["epsc1"], name="ms_eps2")


def emit_wprep(sch, t, v, dram, scr):
    A = sch.add
    for wi_, wname in enumerate(("wfr", "wfi")):
        for kc in range(8):
            N, S = sset(t, kc % 2)
            src32a = dram[wname][128*kc:128*kc+128, 0:512]
            src32b = dram[wname][128*kc:128*kc+128, 512:1024]
            sdma(sch, N["mt"], S["mt"][:], src32a, name=f"wp_l{wi_}{kc}a")
            sdma(sch, N["rt"], S["rt"][:], src32b, name=f"wp_l{wi_}{kc}b")
            if kc % 2 == 0:
                hi, hin = t["bfA"][:, 0:2, :], ["bfA"]
                lo, lon = t["bfB"][:, 0:2, :], ["bfB"]
            else:
                hi, hin = t["pqs"][:, 0:2, :], ["pqs0", "pqs1"]
                lo, lon = t["xfr"][:, 0:2, :], ["xfr0", "xfr1"]
            def hifn(vE, hi=hi, S=S):
                vE.tensor_copy(hi[:, 0, :], S["mt"][:])
                return vE.tensor_copy(hi[:, 1, :], S["rt"][:])
            A("dve", hifn, reads=[N["mt"], N["rt"]] + hin, writes=hin, name=f"wp_hi{wi_}{kc}")
            def lofn(vE, hi=hi, lo=lo, S=S):
                vE.tensor_tensor(lo[:, 0, :], S["mt"][:], hi[:, 0, :], ALU.subtract)
                return vE.tensor_tensor(lo[:, 1, :], S["rt"][:], hi[:, 1, :], ALU.subtract)
            A("dve", lofn, reads=[N["mt"], N["rt"]] + hin + lon, writes=lon, name=f"wp_lo{wi_}{kc}")
            hname = ("wrhi", "wihi")[wi_]; lname = ("wrlo", "wilo")[wi_]
            op = st_sdma(sch, hin, scr[hname][128*kc:128*kc+128, :],
                    hi.rearrange("p a b -> p (a b)"), name=f"wp_sh{wi_}{kc}")
            sch.last_writer[f"hw_{hname}{kc}"] = op
            op = st_sdma(sch, lon, scr[lname][128*kc:128*kc+128, :],
                    lo.rearrange("p a b -> p (a b)"), name=f"wp_sl{wi_}{kc}")
            sch.last_writer[f"hw_{lname}{kc}"] = op


def mbuild_parts(sch, t, v, scr):
    A = sch.add
    wv = {"wrhi": v["wrhi"], "wrlo": v["wrlo"], "wihi": v["wihi"], "wilo": v["wilo"]}

    res = {}

    def load_k(k, tag):
        sl = k % 4
        if res.get(sl) == k:
            return
        res[sl] = k
        for nm in ("wrhi", "wrlo", "wihi", "wilo"):
            op = gdma(sch, [f"{nm}s{sl}"], wv[nm][:, sl:sl+1, :],
                 r128(scr[nm])[:, k:k+1, :], name=f"mb_{tag}_{nm}{k}")
            sch.ops[op].deps = sorted(set(sch.ops[op].deps)
                | {sch.last_writer[f"hw_{nm}{k}"]})

    def split_evac(pb, dsthi, dstlo, ccv, rhv, scale, tag, hkey):
        A("act", lambda sc, pbn=pb: sc.mul(t["pqs"][:, 0, :], t[pbn][:], scale),
          reads=[pb, "pqs0"], writes=["pqs0"], name=f"{tag}_evh")
        A("dve", lambda vE, pbn=pb: vE.scalar_tensor_tensor(
            t["xfr"][:, 0, :], t[pbn][:], scale, t["pqs"][:, 0, :], ALU.mult, ALU.subtract),
          reads=[pb, "pqs0", "xfr0"], writes=["xfr0"], name=f"{tag}_evl")
        op = st_sdma(sch, ["pqs0"], dsthi.rearrange("(a p) r -> p a r", p=128)[:, ccv, 512*rhv:512*rhv+512],
                t["pqs"][:, 0, :], name=f"{tag}_sth")
        sch.last_writer[f"hbm_{hkey}_hi_{ccv}_{rhv}"] = op
        op = st_sdma(sch, ["xfr0"], dstlo.rearrange("(a p) r -> p a r", p=128)[:, ccv, 512*rhv:512*rhv+512],
                t["xfr"][:, 0, :], name=f"{tag}_stl")
        sch.last_writer[f"hbm_{hkey}_lo_{ccv}_{rhv}"] = op

    def stage1_batch(groups, kind, tag, korder):
        for kpos, k in enumerate(korder):
            load_k(k, tag)
            sl = k % 4
            first = kpos == 0
            last = kpos == 7
            slr = [f"{nm}s{sl}" for nm in ("wrhi", "wrlo", "wihi", "wilo")]
            for bi, (cc_, rh) in enumerate(groups):
                if kind == "pt":
                    pb = f"pb{bi}"
                    def mmfn(te, cc_=cc_, rh=rh, pbn=pb, sl=sl, k=k, first=first, last=last):
                        ins = None
                        for wn, cn in (("wrhi", "cmat"), ("wrlo", "cmat"),
                                       ("wihi", "smat"), ("wilo", "smat")):
                            ins = te.matmul(t[pbn][:], wv[wn][:, sl, 128*cc_:128*cc_+128],
                                            v[cn][:, k, 512*rh:512*rh+512],
                                            start=(first and wn == "wrhi"),
                                            stop=(last and wn == "wilo"))
                        return ins
                    A("pe", mmfn, reads=slr + ["cmat", "smat", "H2", pb],
                      writes=[pb], name=f"{tag}_mm{k}_{bi}")
                else:
                    pba, pbb = f"pb{2*bi}", f"pb{2*bi+1}"
                    def mmfa(te, cc_=cc_, rh=rh, pbn=pba, sl=sl, k=k, first=first, last=last):
                        ins = None
                        for wn in ("wihi", "wilo"):
                            ins = te.matmul(t[pbn][:], wv[wn][:, sl, 128*cc_:128*cc_+128],
                                            v["cmat"][:, k, 512*rh:512*rh+512],
                                            start=(first and wn == "wihi"),
                                            stop=(last and wn == "wilo"))
                        return ins
                    def mmfb(te, cc_=cc_, rh=rh, pbn=pbb, sl=sl, k=k, first=first, last=last):
                        ins = None
                        for wn in ("wrhi", "wrlo"):
                            ins = te.matmul(t[pbn][:], wv[wn][:, sl, 128*cc_:128*cc_+128],
                                            v["smat"][:, k, 512*rh:512*rh+512],
                                            start=(first and wn == "wrhi"),
                                            stop=(last and wn == "wrlo"))
                        return ins
                    A("pe", mmfa, reads=slr + ["cmat", "H2", pba], writes=[pba],
                      name=f"{tag}_mma{k}_{bi}")
                    A("pe", mmfb, reads=slr + ["smat", "H2", pbb], writes=[pbb],
                      name=f"{tag}_mmb{k}_{bi}")
        for bi, (cc_, rh) in enumerate(groups):
            if kind == "pt":
                split_evac(f"pb{bi}", scr["pt_hi"], scr["pt_lo"], cc_, rh, 1.0,
                           f"{tag}_e{bi}", "pt")
            else:
                pba, pbb = f"pb{2*bi}", f"pb{2*bi+1}"
                A("act", lambda sc, pbn=pbb: sc.copy(t["fscr"][:], t[pbn][:]),
                  reads=[pbb, "fscr"], writes=["fscr"], name=f"{tag}_cp{bi}")
                A("dve", lambda vE, pbn=pba:
                    vE.tensor_tensor(t["fscr"][:], t["fscr"][:], t[pbn][:], ALU.subtract),
                  reads=[pba, "fscr"], writes=["fscr"], name=f"{tag}_sub{bi}")
                A("act", lambda sc: sc.copy(t["pqs"][:, 0, :], t["fscr"][:]),
                  reads=["fscr", "pqs0"], writes=["pqs0"], name=f"{tag}_hi{bi}")
                A("dve", lambda vE:
                    vE.tensor_tensor(t["xfr"][:, 0, :], t["fscr"][:], t["pqs"][:, 0, :], ALU.subtract),
                  reads=["fscr", "pqs0", "xfr0"], writes=["xfr0"], name=f"{tag}_lo{bi}")
                op = st_sdma(sch, ["pqs0"],
                        scr["qt_hi"].rearrange("(a p) r -> p a r", p=128)[:, cc_, 512*rh:512*rh+512],
                        t["pqs"][:, 0, :], name=f"{tag}_sth{bi}")
                sch.last_writer[f"hbm_qt_hi_{cc_}_{rh}"] = op
                op = st_sdma(sch, ["xfr0"],
                        scr["qt_lo"].rearrange("(a p) r -> p a r", p=128)[:, cc_, 512*rh:512*rh+512],
                        t["xfr"][:, 0, :], name=f"{tag}_stl{bi}")
                sch.last_writer[f"hbm_qt_lo_{cc_}_{rh}"] = op
        return

    groups = [(cc, rh) for cc in range(8) for rh in range(2)]
    korders = [list(range(8)), list(range(7, -1, -1))]
    for b in range(2):
        yield (lambda b=b: stage1_batch(groups[8*b:8*b+8], "pt", f"pt{b}", korders[b % 2]))
    for b in range(4):
        yield (lambda b=b: stage1_batch(groups[4*b:4*b+4], "qt", f"qt{b}", korders[b % 2]))
    def m_part(r):
        for rnd, (ph, qh) in enumerate((("pt_hi", "qt_hi"), ("pt_lo", "qt_lo"))):
            hp = "hi" if rnd == 0 else "lo"
            for hf in range(2):
                cr = list(range(4*hf, 4*hf+4))
                op = sdma(sch, [f"pql0{'ab'[hf]}"], t["pql"][:, 0, 4*hf:4*hf+4, :],
                     r128(scr[ph])[:, 4*hf:4*hf+4, 128*r:128*r+128], name=f"m_ldp{r}_{rnd}{hf}")
                sch.ops[op].deps = sorted(set(sch.ops[op].deps)
                    | {sch.last_writer[f"hbm_pt_{hp}_{cc}_{rh}"] for cc in cr for rh in range(2)})
                op = sdma(sch, [f"pql1{'ab'[hf]}"], t["pql"][:, 1, 4*hf:4*hf+4, :],
                     r128(scr[qh])[:, 4*hf:4*hf+4, 128*r:128*r+128], name=f"m_ldq{r}_{rnd}{hf}")
                sch.ops[op].deps = sorted(set(sch.ops[op].deps)
                    | {sch.last_writer[f"hbm_qt_{hp}_{cc}_{rh}"] for cc in cr for rh in range(2)})
                for oh in range(2):
                    pb = f"pb{(r*2+oh) % 8}"
                    def mfn(te, r=r, oh=oh, pbn=pb, rnd=rnd, hf=hf):
                        ins = None
                        for cc2 in range(4*hf, 4*hf+4):
                            te.matmul(t[pbn][:], t["pql"][:, 0, cc2, :],
                                      v["cmat"][:, cc2, 512*oh:512*oh+512],
                                      start=(rnd == 0 and cc2 == 0), stop=False)
                            ins = te.matmul(t[pbn][:], t["pql"][:, 1, cc2, :],
                                            v["smat"][:, cc2, 512*oh:512*oh+512],
                                            start=False, stop=(rnd == 1 and cc2 == 7))
                        return ins
                    A("pe", mfn, reads=[f"pql0{'ab'[hf]}", f"pql1{'ab'[hf]}", "cmat", "smat", pb],
                      writes=[pb], name=f"m_mm{r}_{oh}_{rnd}{hf}")
        for oh in range(2):
            pb = f"pb{(r*2+oh) % 8}"
            split_evac(pb, scr["m_hi"], scr["m_lo"], r, oh, 1.0/1024., f"m_e{r}_{oh}", "m")
    for r in range(8):
        yield (lambda r=r: m_part(r))


def emit_tables(sch, t, v, dram):
    A = sch.add
    jr = t["c_jrow"][:]
    for n in range(4):
        a_ap = t["lamc"][:, 12+n:13+n]
        b_ap = t["mcol"][:, 8+n:9+n]
        A("act", lambda sc, o=a_ap, i=t["mcol"][:, n:n+1]: sc.activation(o, i, AF.Exp),
          reads=[f"md_lr{n}"], writes=[f"a{n}"], name=f"tb_a{n}")
        A("act", lambda sc, o=b_ap, i=t["mcol"][:, 4+n:5+n]: sc.activation(o, i, AF.Exp),
          reads=[f"md_li{n}"], writes=[f"b{n}"], name=f"tb_b{n}")
        c_ap = t["mcol"][:, 12+n:13+n]
        chi = t["n2sc"][:, 8+n:9+n]
        clo = t["n2sc"][:, 12+n:13+n]
        def csplit(vE, b=b_ap, c=c_ap, chi=chi, clo=clo):
            BAR(vE, "dve", vE.tensor_scalar_mul(c, b, INV2PI))
            BAR(vE, "dve", vE.tensor_scalar(chi, c, 4096.0, MAGIC, ALU.mult, ALU.add))
            BAR(vE, "dve", vE.tensor_scalar_add(chi, chi, -MAGIC))
            BAR(vE, "dve", vE.tensor_scalar_mul(chi, chi, 1.0/4096.0))
            return vE.tensor_tensor(clo, c, chi, ALU.subtract)
        A("dve", csplit, reads=[f"b{n}"], writes=[f"chi{n}"], name=f"tb_csplit{n}")
        def fracA(vE, chi=chi, clo=clo):
            BAR(vE, "dve", vE.tensor_scalar(t["tb1"][:], jr, chi, None, ALU.mult))
            BAR(vE, "dve", vE.tensor_scalar_add(t["tb2"][:], t["tb1"][:], MAGIC))
            BAR(vE, "dve", vE.tensor_scalar_add(t["tb2"][:], t["tb2"][:], -MAGIC))
            BAR(vE, "dve", vE.tensor_tensor(t["tb1"][:], t["tb1"][:], t["tb2"][:], ALU.subtract))
            BAR(vE, "dve", vE.tensor_scalar(t["tb2"][:], jr, clo, None, ALU.mult))
            BAR(vE, "dve", vE.tensor_tensor(t["tb1"][:], t["tb1"][:], t["tb2"][:], ALU.add))
            BAR(vE, "dve", vE.tensor_scalar_add(t["tb2"][:], t["tb1"][:], MAGIC))
            BAR(vE, "dve", vE.tensor_scalar_add(t["tb2"][:], t["tb2"][:], -MAGIC))
            return vE.tensor_tensor(t["tb2"][:], t["tb1"][:], t["tb2"][:], ALU.subtract)
        A("dve", fracA, reads=[f"chi{n}", "c_jrow", "tb1", "tb2"],
          writes=["tb1", "tb2"], name=f"tb_fracA{n}")
        A("act", lambda sc: sc.activation(t["tbs1"][:], t["tb2"][:], AF.Sin, scale=TWO_PI),
          reads=["tb2", "tbs1"], writes=["tbs1"], name=f"tb_sin{n}")
        def fracB(vE):
            BAR(vE, "dve", vE.tensor_scalar_add(t["tb1"][:], t["tb1"][:], 0.25))
            BAR(vE, "dve", vE.tensor_scalar_add(t["tb2"][:], t["tb1"][:], MAGIC))
            BAR(vE, "dve", vE.tensor_scalar_add(t["tb2"][:], t["tb2"][:], -MAGIC))
            return vE.tensor_tensor(t["tb1"][:], t["tb1"][:], t["tb2"][:], ALU.subtract)
        A("dve", fracB, reads=["tb1", "tb2"], writes=["tb1", "tb2"],
          name=f"tb_fracB{n}")
        A("act", lambda sc: sc.activation(t["tb2"][:], t["tb1"][:], AF.Sin, scale=TWO_PI),
          reads=["tb1", "tb2"], writes=["tb2"], name=f"tb_cos{n}")
        na = t["n2sc"][:, 8+n:9+n]
        A("dve", lambda vE, na=na, a=a_ap: vE.tensor_scalar_mul(na, a, -1.0),
          reads=[f"a{n}", f"chi{n}"], writes=[f"na{n}"], name=f"tb_na{n}")
        A("act", lambda sc, na=na: sc.activation(t["tbm"][:], jr, AF.Exp, scale=na),
          reads=[f"na{n}", "c_jrow", "tbm"], writes=["tbm"], name=f"tb_mag{n}")
        A("dve", lambda vE: vE.tensor_tensor(t["tb1"][:], t["tbm"][:], t["tb2"][:], ALU.mult),
          reads=["tbm", "tb2", "tb1"], writes=["tb1"], name=f"tb_srf{n}")
        A("dve", lambda vE: vE.tensor_tensor(t["tbs1"][:], t["tbm"][:], t["tbs1"][:], ALU.mult),
          reads=["tbm", "tbs1"], writes=["tbs1"], name=f"tb_sif{n}")
        A("dve", lambda vE, n=n: vE.tensor_copy(t["srb"][:, n, :], t["tb1"][:]),
          reads=["tb1"], writes=[f"srb{n}"], name=f"tb_srb{n}")
        A("dve", lambda vE, n=n: vE.tensor_copy(t["sib"][:, n, :], t["tbs1"][:]),
          reads=["tbs1"], writes=[f"sib{n}"], name=f"tb_sib{n}")
        def lamcols(vE, n=n):
            vE.tensor_copy(t["lamc"][:, n:n+1], t["tb1"][:, 256:257])
            vE.tensor_copy(t["lamc"][:, 4+n:5+n], t["tbs1"][:, 256:257])
            return vE.tensor_scalar_mul(t["lamc"][:, 8+n:9+n], t["tbs1"][:, 256:257], -1.0)
        A("dve", lamcols, reads=["tb1", "tbs1"], writes=[f"lamc{n}"], name=f"tb_lamc{n}")
        for mch in range(2):
            hi = CC - 128*mch
            for im in range(2):
                srcb = "tb1" if im == 0 else "tbs1"
                dst = "etr" if im == 0 else "eti"
                A("dve", lambda vE, hi=hi, srcb=srcb:
                    vE.tensor_copy(t["fscr"][:, 0:128], t[srcb][:, hi-128:hi][:, ::-1]),
                  reads=[srcb, "fscr"], writes=["fscr"], name=f"et_rv{n}{mch}{im}")
                def trf(te):
                    return te.transpose(t["pb5"][:, 0:128], t["fscr"][:, 0:128], t["c_identf"][:])
                A("pe", trf, reads=["fscr", "c_identf"], writes=["pb5"], name=f"et_tr{n}{mch}{im}")
                def evf(sc, n=n, mch=mch, dst=dst):
                    return sc.copy(t[dst][:, mch, n, :], t["pb5"][:, 0:128])
                A("act", evf, reads=["pb5"], writes=[f"{dst}{n}{mch}"], name=f"et_ev{n}{mch}{im}")
        def wpipe(vE, n=n):
            ur = t["n2sc"][:, 12+n:13+n]
            ui = t["mcol"][:, 12+n:13+n]
            vE.tensor_scalar_add(ur, t["tb1"][:, 1:2], -1.0)
            BAR(vE, "dve", vE.tensor_copy(ui, t["tbs1"][:, 1:2]))
            a = t["lamc"][:, 12+n:13+n]
            b = t["mcol"][:, 8+n:9+n]
            dn = t["tb2"][:, 0:1]
            vE.tensor_tensor(dn, a, a, ALU.mult)
            BAR(vE, "dve", vE.tensor_tensor(t["tb2"][:, 1:2], b, b, ALU.mult))
            BAR(vE, "dve", vE.tensor_tensor(dn, dn, t["tb2"][:, 1:2], ALU.add))
            BAR(vE, "dve", vE.reciprocal(dn, dn))
            wr = t["tb2"][:, 2:3]; s_ = t["tb2"][:, 3:4]
            vE.tensor_tensor(wr, ui, b, ALU.mult)
            BAR(vE, "dve", vE.tensor_tensor(t["tb2"][:, 4:5], ur, a, ALU.mult))
            BAR(vE, "dve", vE.tensor_tensor(wr, wr, t["tb2"][:, 4:5], ALU.subtract))
            BAR(vE, "dve", vE.tensor_tensor(wr, wr, dn, ALU.mult))
            vE.tensor_tensor(s_, ui, a, ALU.mult)
            BAR(vE, "dve", vE.tensor_tensor(t["tb2"][:, 4:5], ur, b, ALU.mult))
            BAR(vE, "dve", vE.tensor_tensor(s_, s_, t["tb2"][:, 4:5], ALU.add))
            BAR(vE, "dve", vE.tensor_tensor(s_, s_, dn, ALU.mult))
            c_raw = t["cctr"][:, n, :]; i_raw = t["nccti"][:, n, :]
            t1 = t["kfcr"][:, 0, :]; t2 = t["kfcr"][:, 1, :]
            vE.tensor_scalar(t1, c_raw, wr, None, ALU.mult)
            BAR(vE, "dve", vE.tensor_scalar(t2, i_raw, s_, None, ALU.mult))
            vE.tensor_scalar(t["st_"][:], c_raw, s_, None, ALU.mult)
            BAR(vE, "dve", vE.tensor_scalar(t["fscr"][:], i_raw, wr, None, ALU.mult))
            BAR(vE, "dve", vE.tensor_tensor(t["cctr"][:, n, :], t1, t2, ALU.add))
            return vE.tensor_tensor(t["nccti"][:, n, :], t["st_"][:], t["fscr"][:], ALU.subtract)
        A("dve", wpipe,
          reads=["tb1", "tbs1", f"a{n}", f"b{n}", "cctr_raw", "ccti_raw", "tb2",
                 "kfcr0", "kfcr1", "st_", "fscr"],
          writes=[f"cct{n}", "kfcr0", "kfcr1", "st_", "fscr", "tb2"], name=f"tb_w{n}")
    emit_tables_k(sch, t)
    A("pe", lambda te: te.matmul(t["pb6"][:], t["ones1"][:], t["n2wb"][:], start=True, stop=True),
      reads=["ones1", "n2wb"], writes=["pb6"], name="n2w_mm")
    A("act", lambda sc: sc.copy(t["n2wt"][:], t["pb6"][:]), reads=["pb6"], writes=["n2wt"],
      name="n2w_ev")
    A("pe", lambda te: te.matmul(t["pb6"][:], t["ones1"][:], t["n2bb"][:], start=True, stop=True),
      reads=["ones1", "n2bb"], writes=["pb6"], name="n2b_mm")
    A("act", lambda sc: sc.copy(t["n2bt"][:], t["pb6"][:]), reads=["pb6"], writes=["n2bt"],
      name="n2b_ev")


def emit_tables_k(sch, t):
    A = sch.add
    for jc in range(2):
        pb = f"pb{jc}"
        def kmm(te, jc=jc, pbn=pb):
            ins = None
            for n in range(4):
                te.matmul(t[pbn][:], t["srb"][:, n, 128*jc:128*jc+128], t["cctr"][:, n, :],
                          start=(n == 0), stop=False)
                ins = te.matmul(t[pbn][:], t["sib"][:, n, 128*jc:128*jc+128], t["nccti"][:, n, :],
                                start=False, stop=(n == 3))
            return ins
        A("pe", kmm, reads=[f"srb{n}" for n in range(4)] + [f"sib{n}" for n in range(4)]
          + [f"cct{n}" for n in range(4)], writes=[pb], name=f"k_mm{jc}")
        A("act", lambda sc, jc=jc, pbn=pb: sc.copy(t["kp"][:, jc, :], t[pbn][:]),
          reads=[pb], writes=[f"kp{jc}", f"pqs{jc}"], name=f"k_ev{jc}")
    A("dve", lambda vE: vE.tensor_tensor(t["kp"][0:1, 0, :], t["kp"][0:1, 0, :],
                                         t["wouA"][0:1, 0, 0:512], ALU.add),
      reads=["kp0", "dbrow", "wou0"], writes=["kp0", "pqs0"], name="k_dtap")
    for fc in range(3):
        for im in range(2):
            mat = "c_fwdc" if im == 0 else "c_fwds"
            dst = "kfcr" if im == 0 else "kfci"
            pb = f"pb{(2 + fc*2 + im) % 8}"
            def kfmm(te, fc=fc, mat=mat, pbn=pb):
                ins = None
                for mch in range(2):
                    ins = te.matmul(t[pbn][:], t[mat][:, mch, 128*fc:128*fc+128],
                                    t["kp"][:, mch, :], start=(mch == 0), stop=(mch == 1))
                return ins
            A("pe", kfmm, reads=["kp0", "kp1", "pqs0", "pqs1", mat], writes=[pb],
              name=f"kf_mm{fc}{im}")
            wr_extra = [f"{dst}{fc}"]
            if dst == "kfcr" and fc in (0, 1):
                wr_extra.append(f"kfcr{fc}_scr")
            A("act", lambda sc, fc=fc, dst=dst, pbn=pb: sc.copy(t[dst][:, fc, :], t[pbn][:]),
              reads=[pb] + (["kfcr0", "kfcr1"] if dst == "kfcr" and fc < 2 else []),
              writes=wr_extra + ([f"{dst}{fc}"] if False else []), name=f"kf_ev{fc}{im}")


def sset(t, par):
    if par == 0:
        n = {"mt": "mt", "rt": "rt_", "st": "st_", "fs": "fscr"}
    else:
        n = {"mt": "mtB", "rt": "rtB", "st": "stB", "fs": "fsB"}
    return n, {k: t[v] for k, v in n.items()}


def emit_p1_pre(sch, t, v, dram, scr, tc):
    A = sch.add
    N, S = sset(t, tc % 2)
    xc = t["xcA"] if tc % 2 == 0 else t["xcB"]
    xcb = f"xc{tc % 2}"
    sdma(sch, xcb, xc[:], r128(dram["x0"])[:, :, 512*tc:512*tc+512], name=f"p1a_ld{tc}")
    A("dve", lambda vE, xc=xc: vE.tensor_copy(t["bfA"][:], xc[:]),
      reads=[xcb], writes=["bfA"], name=f"p1a_cast{tc}")
    A("act", lambda sc, xc=xc: sc.activation(t["bfB"][:], xc[:], AF.Square),
      reads=[xcb], writes=["bfB"], name=f"p1a_sq{tc}")
    def stats1(te):
        ins = None
        for k in range(8):
            ins = te.matmul(t["pb6"][:], t["onesbf"][:], t["bfA"][:, k, :],
                            start=(k == 0), stop=(k == 7))
        return ins
    A("pe", stats1, reads=["bfA", "onesbf"], writes=["pb6"], name=f"p1a_s1_{tc}")
    def stats2(te):
        ins = None
        for k in range(8):
            ins = te.matmul(t["pb7"][:], t["onesbf"][:], t["bfB"][:, k, :],
                            start=(k == 0), stop=(k == 7))
        return ins
    A("pe", stats2, reads=["bfB", "onesbf"], writes=["pb7"], name=f"p1a_s2_{tc}")
    A("act", lambda sc: sc.copy(S["mt"][:], t["pb6"][:]), reads=["pb6"], writes=[N["mt"]],
      name=f"p1a_m{tc}")
    def varfn(vE):
        BAR(vE, "dve", vE.tensor_tensor(S["rt"][:], S["mt"][:], S["mt"][:], ALU.mult))
        return vE.tensor_tensor(S["rt"][:], t["pb7"][:], S["rt"][:], ALU.subtract)
    A("dve", varfn, reads=[N["mt"], "pb7", N["rt"]], writes=[N["rt"]], name=f"p1a_var{tc}")
    def rfn(sc):
        BAR(sc, "act", sc.activation(S["st"][:], S["rt"][:], AF.Ln, bias=t["epsc"][:, 0:1]))
        return sc.activation(S["rt"][:], S["st"][:], AF.Exp, scale=-0.5)
    A("act", rfn, reads=[N["rt"], "epsc0", N["st"]], writes=[N["rt"], N["st"]], name=f"p1a_r{tc}")
    def ufn(vE, xc=xc):
        ins = None
        for k in range(8):
            ins = vE.tensor_tensor(t["bfB"][:, k, :], xc[:, k, :], S["mt"][:], ALU.subtract)
        BAR(vE, "dve", ins)
        for k in range(8):
            ins = vE.tensor_tensor(t["bfA"][:, k, :], t["bfB"][:, k, :], S["rt"][:], ALU.mult)
        return ins
    A("dve", ufn, reads=[xcb, N["mt"], N["rt"]], writes=["bfA", "bfB"], name=f"p1a_u{tc}")
    op = st_sdma(sch, ["bfA"], r128(scr["u1"])[:, :, 512*tc:512*tc+512], t["bfA"][:], name=f"p1a_sp{tc}")
    sch.last_writer[f"hu1_{tc}"] = op


def emit_p1_post(sch, t, v, dram, scr, tc):
    A = sch.add
    N, S = sset(t, tc % 2)
    xc = t["xcA"] if tc % 2 == 0 else t["xcB"]
    xcb = f"xc{tc % 2}"
    sdma(sch, xcb, xc[:], r128(dram["x0"])[:, :, 512*tc:512*tc+512], name=f"p1b_ldx{tc}")
    sdma(sch, "bfB", t["bfB"][:], r128(scr["u1"])[:, :, 512*tc:512*tc+512], name=f"p1b_ldu{tc}")
    sch.ops[-1].deps = sorted(set(sch.ops[-1].deps) | {sch.last_writer[f"hu1_{tc}"]})
    for phase, mkey in enumerate(("m_hi", "m_lo")):
        for pos, oc in enumerate((4, 5, 6, 7, 0, 1, 2, 3)):
            sl = (8*phase + pos) % 3
            mb = [f"pql{sl}a", f"pql{sl}b"]
            op = sdma(sch, mb, t["pql"][:, sl, :, :],
                 r128(scr[mkey])[:, :, 128*oc:128*oc+128], name=f"p1b_ldm{tc}_{phase}_{oc}")
            hp = "hi" if phase == 0 else "lo"
            sch.ops[op].deps = sorted(set(sch.ops[op].deps)
                | {sch.last_writer[f"hbm_m_{hp}_{r}_{h}"] for r in range(8) for h in range(2)})
            pb = f"pb{oc}"
            def ffn_mm(te, oc=oc, pbn=pb, phase=phase, sl=sl):
                ins = None
                for k in range(8):
                    ins = te.matmul(t[pbn][:], t["pql"][:, sl, k, :], t["bfB"][:, k, :],
                                    start=(phase == 0 and k == 0), stop=False)
                if phase == 1:
                    ins = te.matmul(t[pbn][:], t["fbrow"][0:1, 128*oc:128*oc+128],
                                    t["onesrow"][:], start=False, stop=True)
                return ins
            A("pe", ffn_mm, reads=["bfB", "fbrow", "onesrow"] + mb + ([pb] if phase else []),
              writes=[pb], name=f"p1b_ffn{tc}_{phase}_{oc}")
    for oc in (4, 5, 6, 7, 0, 1, 2, 3):
        pb = f"pb{oc}"
        A("dve", lambda vE, oc=oc, pbn=pb, xc=xc:
            vE.scalar_tensor_tensor(xc[:, oc, :], t[pbn][:], 0.5, xc[:, oc, :], ALU.mult, ALU.add),
          reads=[pb, xcb], writes=[xcb], name=f"p1b_x1_{tc}_{oc}")
    A("dve", lambda vE, xc=xc: vE.tensor_copy(t["bfA"][:], xc[:]),
      reads=[xcb], writes=["bfA"], name=f"p1b_cast{tc}")
    A("act", lambda sc, xc=xc: sc.activation(t["bfB"][:], xc[:], AF.Square),
      reads=[xcb, "bfB"], writes=["bfB"], name=f"p1b_sq{tc}")
    def stats1(te):
        ins = None
        for k in range(8):
            ins = te.matmul(t["pb6"][:], t["onesbf"][:], t["bfA"][:, k, :],
                            start=(k == 0), stop=(k == 7))
        return ins
    A("pe", stats1, reads=["bfA", "onesbf"], writes=["pb6"], name=f"p1b_s1_{tc}")
    def stats2(te):
        ins = None
        for k in range(8):
            ins = te.matmul(t["pb7"][:], t["onesbf"][:], t["bfB"][:, k, :],
                            start=(k == 0), stop=(k == 7))
        return ins
    A("pe", stats2, reads=["bfB", "onesbf"], writes=["pb7"], name=f"p1b_s2_{tc}")
    A("act", lambda sc: sc.copy(S["mt"][:], t["pb6"][:]), reads=["pb6"], writes=[N["mt"]],
      name=f"p1b_m{tc}")
    def varfn(vE):
        BAR(vE, "dve", vE.tensor_tensor(S["rt"][:], S["mt"][:], S["mt"][:], ALU.mult))
        return vE.tensor_tensor(S["fs"][:], t["pb7"][:], S["rt"][:], ALU.subtract)
    A("dve", varfn, reads=[N["mt"], "pb7", N["rt"], N["fs"]], writes=[N["fs"], N["rt"]],
      name=f"p1b_var{tc}")
    def rsfn(sc):
        BAR(sc, "act", sc.activation(S["st"][:], S["fs"][:], AF.Ln, bias=t["epsc"][:, 0:1]))
        BAR(sc, "act", sc.activation(S["rt"][:], S["st"][:], AF.Exp, scale=-0.5))
        BAR(sc, "act", sc.activation(S["st"][:], S["fs"][:], AF.Ln, bias=t["epsc"][:, 1:2], scale=1.0+EPS))
        return sc.activation(S["st"][:], S["st"][:], AF.Exp, scale=-0.5)
    A("act", rsfn, reads=[N["fs"], "epsc0", "epsc1", N["st"], N["rt"]], writes=[N["rt"], N["st"]],
      name=f"p1b_rs{tc}")
    def vfnA(vE, xc=xc):
        ins = None
        for k in range(8):
            BAR(vE, "dve", vE.tensor_tensor(S["fs"][:], xc[:, k, :], S["mt"][:], ALU.subtract))
            BAR(vE, "dve", vE.tensor_tensor(t["bfA"][:, k, :], S["fs"][:], S["st"][:], ALU.mult))
            ins = vE.tensor_scalar(t["bfA"][:, k, :], t["bfA"][:, k, :], t["c_n1w"][:, k:k+1],
                                   t["c_n1b"][:, k:k+1], ALU.mult, ALU.add)
        return ins
    A("dve", vfnA, reads=[xcb, N["mt"], N["st"], "c_n1w", "c_n1b", N["fs"]],
      writes=["bfA", N["fs"]], name=f"p1b_vA{tc}")
    def vfnC(vE):
        BAR(vE, "dve", vE.tensor_tensor(S["st"][:], S["mt"][:], S["rt"][:], ALU.mult))
        return vE.tensor_scalar_add(S["rt"][:], S["rt"][:], 1.0)
    A("dve", vfnC, reads=[N["mt"], N["rt"], N["st"]], writes=[N["rt"], N["st"]],
      name=f"p1b_vC{tc}")
    def vfnB(vE, xc=xc):
        ins = None
        for k in range(8):
            ins = vE.tensor_tensor(xc[:, k, :], xc[:, k, :], S["rt"][:], ALU.mult)
        BAR(vE, "pool", ins)
        for k in range(8):
            ins = vE.tensor_tensor(xc[:, k, :], xc[:, k, :], S["st"][:], ALU.subtract)
        return ins
    A("pool", vfnB, reads=[xcb, N["rt"], N["st"], "bfA"], writes=[xcb],
      name=f"p1b_vB{tc}")
    op = st_sdma(sch, [xcb], r128(scr["x1v"])[:, :, 512*tc:512*tc+512], xc[:], name=f"p1b_spx{tc}")
    sch.last_writer[f"hx1v_{tc}"] = op
    op = st_sdma(sch, ["bfA"], r128(scr["xln"])[:, :, 512*tc:512*tc+512], t["bfA"][:], name=f"p1b_spl{tc}")
    sch.last_writer[f"hxln_{tc}"] = op
    tkeys = ("mt", "rt", "st", "fs")
    par = tc % 2
    m4 = t["n2sc"][:, 8*par:8*par+4]
    q4 = t["n2sc"][:, 8*par+4:8*par+8]
    for tt in range(4):
        pb = f"pb{tt}"
        def xd_mm(te, tt=tt, pbn=pb):
            ins = None
            for k in range(8):
                ins = te.matmul(t[pbn][:], t["bfA"][:, k, 128*tt:128*tt+128], t["c_wdin"][:, k, :],
                                start=(k == 0), stop=(k == 7))
            return ins
        A("pe", xd_mm, reads=["bfA", "c_wdin"], writes=[pb], name=f"p1b_xd{tc}_{tt}")
    for tt in range(4):
        A("act", lambda sc, tt=tt: sc.copy(S[tkeys[tt]][:], t[f"pb{tt}"][:]),
          reads=[f"pb{tt}"], writes=[N[tkeys[tt]]], name=f"p1b_xde{tc}_{tt}")
    def n2mfn(vE):
        ins = None
        for tt in range(4):
            ins = vE.reduce_sum(m4[:, tt:tt+1], S[tkeys[tt]][:], axis=AX.X)
        BAR(vE, "dve", ins)
        return vE.tensor_scalar_mul(m4, m4, 1.0/512.0)
    A("dve", n2mfn, reads=[N[k] for k in tkeys], writes=[f"n2m{par}"], name=f"p1b_n2m{tc}")
    def n2cfn(vE):
        ins = None
        for tt in range(4):
            ins = vE.tensor_scalar(S[tkeys[tt]][:], S[tkeys[tt]][:], m4[:, tt:tt+1],
                                   None, ALU.subtract)
        return ins
    A("dve", n2cfn, reads=[f"n2m{par}"] + [N[k] for k in tkeys],
      writes=[N[k] for k in tkeys], name=f"p1b_n2c{tc}")
    for half in range(2):
        def n2sqfn(vE, half=half):
            ins = None
            for j in range(2):
                tt = 2*half + j
                ins = vE.tensor_tensor(t["pqs"][:, j, :], S[tkeys[tt]][:], S[tkeys[tt]][:],
                                       ALU.mult)
            return ins
        A("pool", n2sqfn, reads=[N[k] for k in tkeys] + ["pqs0", "pqs1"],
          writes=["pqs0", "pqs1"], name=f"p1b_n2sq{tc}_{half}")
        def n2qfn(vE, half=half):
            ins = None
            for j in range(2):
                tt = 2*half + j
                ins = vE.reduce_sum(q4[:, tt:tt+1], t["pqs"][:, j, :], axis=AX.X)
            return ins
        A("dve", n2qfn, reads=["pqs0", "pqs1"], writes=[f"n2q{par}", "pqs0", "pqs1"],
          name=f"p1b_n2q{tc}_{half}")
    def n2qsc(vE):
        return vE.tensor_scalar_mul(q4, q4, 1.0/512.0)
    A("dve", n2qsc, reads=[f"n2q{par}"], writes=[f"n2q{par}"], name=f"p1b_n2qs{tc}")
    def n2rfn(sc):
        BAR(sc, "act", sc.activation(q4, q4, AF.Ln, bias=t["epsc"][:, 0:1]))
        return sc.activation(q4, q4, AF.Exp, scale=-0.5)
    A("act", n2rfn, reads=[f"n2q{par}", "epsc0"], writes=[f"n2q{par}"], name=f"p1b_n2r{tc}")
    def xnfn(vE):
        ins = None
        for tt in range(4):
            ins = vE.tensor_scalar(S[tkeys[tt]][:], S[tkeys[tt]][:], q4[:, tt:tt+1],
                                   None, ALU.mult)
        BAR(vE, "dve", ins)
        for tt in range(4):
            ins = vE.tensor_tensor(S[tkeys[tt]][:], S[tkeys[tt]][:], t["n2wt"][:], ALU.mult)
        BAR(vE, "dve", ins)
        for tt in range(4):
            ins = vE.tensor_tensor(v["xn"][:, 4*tc+tt, :], S[tkeys[tt]][:], t["n2bt"][:], ALU.add)
        return ins
    A("dve", xnfn, reads=[f"n2q{par}", "n2wt", "n2bt"] + [N[k] for k in tkeys],
      writes=[f"xn{4*tc+tt}" for tt in range(4)] + ["H1"] + [N[k] for k in tkeys],
      name=f"p1b_xn{tc}")


def emit_p2_seq(sch, t, v, sq):
    A = sch.add
    A("dve", lambda vE: vE.memset(t["xcB"][:], 0.0), writes=["xc1", "str_", "sti"],
      name=f"p2_rst{sq}")
    A("dve", lambda vE: vE.memset(t["bfB"][:], 0.0), writes=["bfB", "strb", "nstib"],
      name=f"p2_rstb{sq}")
    for i in range(NCH):
        N, S = sset(t, i % 2)
        dtmv = [S["mt"], S["rt"]]
        dtmn = [N["mt"], N["rt"]]
        slots = (16*sq + 2*i, 16*sq + 2*i + 1)
        xnslots = [f"xn{s}" for s in slots]
        if i % 2 == 0:
            yfr_s = [t["xfr"][:, fc, :] for fc in range(3)]
            yfrn = [f"xfr{fc}" for fc in range(3)]
            yfi_s = [t["yfi"][:, fc, :] for fc in range(3)]
            yfin = [f"yfi{fc}" for fc in range(3)]
            cpl = []
        else:
            yfr_s = [t["bfA"][:, fc, :] for fc in range(3)]
            yfrn = [f"xfB{fc}" for fc in range(3)]
            yfi_s = [t["bfA"][:, 3 + fc, :] for fc in range(3)]
            yfin = [f"yfB{fc}" for fc in range(3)]
            cpl = ["bfA"]
        for n in range(4):
            for im in range(2):
                mat = "etr" if im == 0 else "eti"
                pb = f"pb{n*2+im}"
                def emm(te, n=n, mat=mat, pbn=pb, slots=slots):
                    ins = None
                    for mch in range(2):
                        ins = te.matmul(t[pbn][:], t[mat][:, mch, n, :],
                                        v["xn"][:, slots[mch], :], start=(mch == 0), stop=(mch == 1))
                    return ins
                A("pe", emm, reads=xnslots + [f"{mat}{n}0", f"{mat}{n}1"], writes=[pb],
                  name=f"p2e{sq}_{i}_{n}{im}")
                phs = v["phR"] if im == 0 else v["phI"]
                cplw = "wxg0" if im == 0 else "wxg1"
                A("act", lambda sc, n=n, pbn=pb, phs=phs: sc.copy(phs[:, 512*n:512*n+512], t[pbn][:]),
                  reads=[pb], writes=[f"ph{im}{n}", cplw], name=f"p2pe{sq}_{i}_{n}{im}")
        for n in range(4):
            eng = "dve" if n % 2 == 0 else "pool"
            pbr, pbi = f"pb{2*n}", f"pb{2*n+1}"
            sa = S["fs"] if eng == "dve" else S["st"]
            na = N["fs"] if eng == "dve" else N["st"]
            sb = S["mt"] if eng == "dve" else S["rt"]
            nb = N["mt"] if eng == "dve" else N["rt"]
            phr = v["phR"][:, 512*n:512*n+512]
            phi = v["phI"][:, 512*n:512*n+512]
            if eng == "dve":
                def foldR(vE, n=n, sa=sa, pbr=pbr, pbi=pbi):
                    vE.tensor_tensor(v["psir"][:, n, :], t["cctr"][:, n, :], t[pbr][:], ALU.mult)
                    BAR(vE, "dve", vE.tensor_tensor(sa[:], t["nccti"][:, n, :], t[pbi][:], ALU.mult))
                    return vE.tensor_tensor(v["psir"][:, n, :], v["psir"][:, n, :], sa[:], ALU.add)
                A("dve", foldR, reads=[pbr, pbi, f"cct{n}", na],
                  writes=[f"psir{n}", na, "xc0"], name=f"p2dR{sq}_{i}_{n}")
                def foldI(vE, n=n, sb=sb, phr=phr, phi=phi):
                    vE.tensor_tensor(v["psii"][:, n, :], t["cctr"][:, n, :], phi, ALU.mult)
                    BAR(vE, "pool", vE.tensor_tensor(sb[:], t["nccti"][:, n, :], phr, ALU.mult))
                    return vE.tensor_tensor(v["psii"][:, n, :], v["psii"][:, n, :], sb[:], ALU.subtract)
                A("pool", foldI,
                  reads=[f"ph0{n}", f"ph1{n}", f"cct{n}", nb, "wxg0", "wxg1"],
                  writes=[f"psii{n}", nb, "xc0"], name=f"p2dI{sq}_{i}_{n}")
            else:
                def foldR(vE, n=n, sa=sa, phr=phr, phi=phi):
                    vE.tensor_tensor(v["psir"][:, n, :], t["cctr"][:, n, :], phr, ALU.mult)
                    BAR(vE, "pool", vE.tensor_tensor(sa[:], t["nccti"][:, n, :], phi, ALU.mult))
                    return vE.tensor_tensor(v["psir"][:, n, :], v["psir"][:, n, :], sa[:], ALU.add)
                A("pool", foldR, reads=[f"ph0{n}", f"ph1{n}", f"cct{n}", na, "wxg0", "wxg1"],
                  writes=[f"psir{n}", na, "xc0"], name=f"p2dR{sq}_{i}_{n}")
                def foldI(vE, n=n, sb=sb, pbr=pbr, pbi=pbi):
                    vE.tensor_tensor(v["psii"][:, n, :], t["cctr"][:, n, :], t[pbi][:], ALU.mult)
                    BAR(vE, "dve", vE.tensor_tensor(sb[:], t["nccti"][:, n, :], t[pbr][:], ALU.mult))
                    return vE.tensor_tensor(v["psii"][:, n, :], v["psii"][:, n, :], sb[:], ALU.subtract)
                A("dve", foldI, reads=[pbr, pbi, f"cct{n}", nb],
                  writes=[f"psii{n}", nb, "xc0"], name=f"p2dI{sq}_{i}_{n}")
        for fc in range(3):
            for im in range(2):
                mat = "c_fwdc" if im == 0 else "c_fwds"
                pb = f"pb{fc*2+im}"
                def fmm(te, fc=fc, mat=mat, pbn=pb, slots=slots):
                    ins = None
                    for mch in range(2):
                        ins = te.matmul(t[pbn][:], t[mat][:, mch, 128*fc:128*fc+128],
                                        v["xn"][:, slots[mch], :], start=(mch == 0), stop=(mch == 1))
                    return ins
                A("pe", fmm, reads=xnslots + [mat], writes=[pb], name=f"p2f{sq}_{i}_{fc}{im}")
        for j, pbn in enumerate(("pb2", "pb3")):
            A("act", lambda sc, j=j, pbn=pbn: sc.copy(v["xf1"][:, 512*j:512*j+512], t[pbn][:]),
              reads=[pbn], writes=[f"xf1{j}", "wgt0"], name=f"p2xe{sq}_{i}_{j}")
        for fc in range(3):
            eng = "dve" if fc % 2 == 0 else "pool"
            pbr, pbi = f"pb{2*fc}", f"pb{2*fc+1}"
            stt = S["st"] if eng == "dve" else S["fs"]
            stn = N["st"] if eng == "dve" else N["fs"]
            if eng == "dve":
                xr, xi = t[pbr][:], t[pbi][:]
                xrd = [pbr, pbi]
            else:
                xr, xi = v["xf1"][:, 0:512], v["xf1"][:, 512:1024]
                xrd = ["xf10", "xf11", "wgt0"]
            def prodI(vE, fc=fc, eng=eng, stt=stt, xr=xr, xi=xi, yfi_s=yfi_s):
                vE.tensor_tensor(yfi_s[fc], xr, t["kfci"][:, fc, :], ALU.mult)
                BAR(vE, eng, vE.tensor_tensor(stt[:], xi, t["kfcr"][:, fc, :], ALU.mult))
                return vE.tensor_tensor(yfi_s[fc], yfi_s[fc], stt[:], ALU.add)
            A(eng, prodI, reads=xrd + [f"kfcr{fc}", f"kfci{fc}", stn],
              writes=[yfin[fc], stn] + cpl, name=f"p2pI{sq}_{i}_{fc}")
            def prodR(vE, fc=fc, eng=eng, stt=stt, xr=xr, xi=xi, yfr_s=yfr_s):
                vE.tensor_tensor(yfr_s[fc], xr, t["kfcr"][:, fc, :], ALU.mult)
                BAR(vE, eng, vE.tensor_tensor(stt[:], xi, t["kfci"][:, fc, :], ALU.mult))
                return vE.tensor_tensor(yfr_s[fc], yfr_s[fc], stt[:], ALU.subtract)
            A(eng, prodR, reads=xrd + [f"kfcr{fc}", f"kfci{fc}", stn],
              writes=[yfrn[fc], stn] + cpl, name=f"p2pR{sq}_{i}_{fc}")
        for lt in range(2):
            pb = f"pb{6+lt}"
            def ymm2(te, lt=lt, pbn=pb, first=(i == 0), yfr_s=yfr_s, yfi_s=yfi_s):
                ins = None
                seqm = []
                for fc in range(3):
                    seqm.append(("c_invc", fc, yfr_s[fc]))
                    seqm.append(("c_invs", fc, yfi_s[fc]))
                if not first:
                    for n in range(4):
                        seqm.append(("srb", n, None))
                        seqm.append(("sib", n, None))
                for ix, (mat, c, rh) in enumerate(seqm):
                    if mat in ("c_invc", "c_invs"):
                        lhs = t[mat][:, c, 128*lt:128*lt+128]
                        rhs = rh
                    else:
                        lhs = t[mat][:, c, 1+128*lt:129+128*lt]
                        rhs = v["strb"][:, c, :] if mat == "srb" else v["nstib"][:, c, :]
                    ins = te.matmul(t[pbn][:], lhs, rhs, start=(ix == 0), stop=(ix == len(seqm)-1))
                return ins
            rdy = yfrn + yfin + cpl
            if i > 0:
                rdy += ["strb", "nstib"] + [f"srb{n}" for n in range(4)] + [f"sib{n}" for n in range(4)]
            A("pe", ymm2, reads=rdy + ["c_invc", "c_invs"], writes=[pb], name=f"p2y{sq}_{i}_{lt}")
            A("act", lambda sc, lt=lt, pbn=pb, dtmv=dtmv: sc.copy(dtmv[lt][:], t[pbn][:]),
              reads=[pb], writes=[dtmn[lt]],
              name=f"p2ye{sq}_{i}_{lt}")
        for n in range(4):
            seng = "dve"
            def scanf(vE, n=n, seng=seng):
                lcr = t["lamc"][:, n:n+1]
                lci = t["lamc"][:, 4+n:5+n]
                nlci = t["lamc"][:, 8+n:9+n]
                BAR(vE, seng, vE.scalar_tensor_tensor(v["psii"][:, n, :], v["sti"][:, n, :], lcr,
                                        v["psii"][:, n, :], ALU.mult, ALU.add))
                BAR(vE, seng, vE.scalar_tensor_tensor(v["psii"][:, n, :], v["str_"][:, n, :], lci,
                                        v["psii"][:, n, :], ALU.mult, ALU.add))
                BAR(vE, seng, vE.scalar_tensor_tensor(v["psir"][:, n, :], v["str_"][:, n, :], lcr,
                                        v["psir"][:, n, :], ALU.mult, ALU.add))
                BAR(vE, seng, vE.scalar_tensor_tensor(v["str_"][:, n, :], v["sti"][:, n, :], nlci,
                                        v["psir"][:, n, :], ALU.mult, ALU.add))
                return vE.tensor_copy(v["sti"][:, n, :], v["psii"][:, n, :])
            A(seng, scanf, reads=[f"psir{n}", f"psii{n}", f"lamc{n}", "str_", "sti",
                                   "xc0", "xc1"],
              writes=["str_", "sti", f"psir{n}", f"psii{n}", "xc0", "xc1"],
              name=f"p2s{sq}_{i}_{n}")
        def mirf(vE):
            vE.tensor_copy(v["strb"][:], v["str_"][:])
            return vE.tensor_scalar_mul(v["nstib"][:], v["sti"][:], -1.0)
        A("dve", mirf, reads=["str_", "sti", "xc1", "bfB"], writes=["strb", "nstib", "bfB"],
          name=f"p2mir{sq}_{i}")
        for lt in range(2):
            for q in range(4):
                pbt = "pb4" if q % 2 == 0 else "pb5"
                def trf(te, lt=lt, q=q, dtmv=dtmv, pbt=pbt):
                    return te.transpose(t[pbt][:, 0:128], dtmv[lt][:, 128*q:128*q+128],
                                        t["c_identf"][:])
                A("pe", trf, reads=[dtmn[lt], "c_identf"], writes=[pbt],
                  name=f"p2t{sq}_{i}_{lt}{q}")
                gt = 2048*sq + 256*i + 128*lt
                def evf(sc, q=q, gt=gt, pbt=pbt):
                    return sc.copy(v["dfm"][:, q, gt:gt+128], t[pbt][:, 0:128])
                A("act", evf, reads=[pbt], writes=[f"dfm{q}_{gt//512}", "H2"],
                  name=f"p2te{sq}_{i}_{lt}{q}")


def emit_p3_chunk(sch, t, v, dram, scr, tc):
    A = sch.add
    xc = t["xcA"] if tc % 2 == 0 else t["xcB"]
    xcb = f"xc{tc % 2}"
    sdma(sch, [xcb], xc[:], r128(scr["x1v"])[:, :, 512*tc:512*tc+512], name=f"p3_ldx{tc}")
    sch.ops[-1].deps = sorted(set(sch.ops[-1].deps) | {sch.last_writer[f"hx1v_{tc}"]})
    sdma(sch, ["bfB", "strb", "nstib"], t["bfB"][:],
         r128(scr["xln"])[:, :, 512*tc:512*tc+512], name=f"p3_ldl{tc}")
    sch.ops[-1].deps = sorted(set(sch.ops[-1].deps) | {sch.last_writer[f"hxln_{tc}"]})
    dfm_bufs = [f"dfm{q}_{tc}" for q in range(4)]
    for s in range(16):
        wxg = t["wxgA"] if s % 2 == 0 else t["wxgB"]
        wxgb = f"wxg{s % 2}"
        op = sdma(sch, wxgb, wxg[:], r128(scr["wxgb"])[:, :, 256*s:256*s+256], name=f"p3_lwxg{tc}_{s}")
        sch.ops[op].deps = sorted(set(sch.ops[op].deps) | {sch.last_writer["cw_wxg"]})
        wgt = t["wgtA"] if s % 2 == 0 else t["wgtB"]
        wgtb = f"wgt{s % 2}"
        op = sdma(sch, wgtb, wgt[:], r128(scr["wgtb"])[:, :, 256*s:256*s+256], name=f"p3_lwgt{tc}_{s}")
        sch.ops[op].deps = sorted(set(sch.ops[op].deps) | {sch.last_writer["cw_wgate"]})
        for j in range(2):
            pbx = f"pb{j}"
            def xgmm(te, j=j, wxg=wxg, pbn=pbx):
                ins = None
                for k in range(8):
                    ins = te.matmul(t[pbn][:], wxg[:, k, 128*j:128*j+128], t["bfB"][:, k, :],
                                    start=(k == 0), stop=(k == 7))
                return ins
            A("pe", xgmm, reads=[wxgb, "bfB"], writes=[pbx], name=f"p3xg{tc}_{s}_{j}")
            slot = 2*s + j
            A("act", lambda sc, slot=slot, pbn=pbx: sc.copy(v["xgP"][:, slot, :], t[pbn][:]),
              reads=[pbx], writes=[f"xgP{slot}", "H1"], name=f"p3xge{tc}_{s}_{j}")
            pbg = f"pb{2+j}"
            def gmm(te, j=j, wgt=wgt, pbn=pbg, tc=tc):
                ins = None
                for hc in range(4):
                    ins = te.matmul(t[pbn][:], wgt[:, hc, 128*j:128*j+128],
                                    v["dfm"][:, hc, 512*tc:512*tc+512],
                                    start=(hc == 0), stop=(hc == 3))
                return ins
            A("pe", gmm, reads=[wgtb] + dfm_bufs, writes=[pbg], name=f"p3g{tc}_{s}_{j}")
            A("dve", lambda vE, slot=slot, pbn=pbg:
                vE.tensor_tensor(v["xgP"][:, slot, :], t[pbn][:], v["xgP"][:, slot, :], ALU.mult),
              reads=[pbg, f"xgP{slot}"], writes=[f"xgP{slot}"], name=f"p3P{tc}_{s}_{j}")
    for s in range(32):
        wou = t["wouA"] if s % 2 == 0 else t["wouB"]
        woub = f"wou{s % 2}"
        op = sdma(sch, [woub], wou[:],
             r128(scr["woub"])[:, s:s+1, :], name=f"p3_lwou{tc}_{s}")
        sch.ops[op].deps = sorted(set(sch.ops[op].deps) | {sch.last_writer["cw_wout"]})
        for oc in range(8):
            pb = f"pb{oc}"
            def betamm(te, s=s, oc=oc, wou=wou, pbn=pb):
                return te.matmul(t[pbn][:], wou[:, 0, 128*oc:128*oc+128],
                                 v["xgP"][:, s, :], start=(s == 0), stop=False)
            A("pe", betamm, reads=[woub, f"xgP{s}"] + ([] if s == 0 else [pb]),
              writes=[pb] if s == 0 else [], name=f"p3b{tc}_{s}_{oc}")
    for oc in range(8):
        pb = f"pb{oc}"
        def bstop(te, oc=oc, pbn=pb):
            return te.matmul(t[pbn][:], t["borow"][0:1, 128*oc:128*oc+128], t["onesrow"][:],
                             start=False, stop=True)
        A("pe", bstop, reads=["borow", "onesrow", pb], writes=[pb], name=f"p3bs{tc}_{oc}")
    for oc in range(8):
        pb = f"pb{oc}"
        A("dve", lambda vE, oc=oc, pbn=pb, xc=xc:
            vE.tensor_tensor(xc[:, oc, :], t[pbn][:], xc[:, oc, :], ALU.add),
          reads=[pb, xcb], writes=[xcb], name=f"p3x2_{tc}_{oc}")


def emit_p4_chunk(sch, t, v, dram, scr, yout, tc):
    A = sch.add
    N, S = sset(t, tc % 2)
    xc = t["xcA"] if tc % 2 == 0 else t["xcB"]
    xcb = f"xc{tc % 2}"
    A("dve", lambda vE, xc=xc: vE.tensor_copy(t["bfA"][:], xc[:]),
      reads=[xcb, "bfA"], writes=["bfA"], name=f"p4_cast{tc}")
    A("act", lambda sc, xc=xc: sc.activation(t["bfB"][:], xc[:], AF.Square),
      reads=[xcb, "bfB"], writes=["bfB", "strb", "nstib"], name=f"p4_sq{tc}")
    def stats1(te):
        ins = None
        for k in range(8):
            ins = te.matmul(t["pb6"][:], t["onesbf"][:], t["bfA"][:, k, :],
                            start=(k == 0), stop=(k == 7))
        return ins
    A("pe", stats1, reads=["bfA", "onesbf"], writes=["pb6"], name=f"p4_s1_{tc}")
    def stats2(te):
        ins = None
        for k in range(8):
            ins = te.matmul(t["pb7"][:], t["onesbf"][:], t["bfB"][:, k, :],
                            start=(k == 0), stop=(k == 7))
        return ins
    A("pe", stats2, reads=["bfB", "onesbf"], writes=["pb7"], name=f"p4_s2_{tc}")
    A("act", lambda sc: sc.copy(S["mt"][:], t["pb6"][:]), reads=["pb6"],
      writes=[N["mt"]], name=f"p4_m{tc}")
    def varfn(vE):
        BAR(vE, "dve", vE.tensor_tensor(S["rt"][:], S["mt"][:], S["mt"][:], ALU.mult))
        return vE.tensor_tensor(S["rt"][:], t["pb7"][:], S["rt"][:], ALU.subtract)
    A("dve", varfn, reads=[N["mt"], "pb7", N["rt"]], writes=[N["rt"]], name=f"p4_var{tc}")
    def rfn(sc):
        BAR(sc, "act", sc.activation(S["st"][:], S["rt"][:], AF.Ln, bias=t["epsc"][:, 0:1]))
        return sc.activation(S["rt"][:], S["st"][:], AF.Exp, scale=-0.5)
    A("act", rfn, reads=[N["rt"], "epsc0", N["st"]], writes=[N["rt"], N["st"]], name=f"p4_r{tc}")
    def ufn(vE, xc=xc):
        ins = None
        for k in range(8):
            ins = vE.tensor_tensor(t["bfB"][:, k, :], xc[:, k, :], S["mt"][:], ALU.subtract)
        BAR(vE, "dve", ins)
        for k in range(8):
            ins = vE.tensor_tensor(t["bfB"][:, k, :], t["bfB"][:, k, :], S["rt"][:], ALU.mult)
        return ins
    A("dve", ufn, reads=[xcb, N["mt"], N["rt"], "bfB"], writes=["bfB"], name=f"p4_u{tc}")
    for phase, mkey in enumerate(("m_hi", "m_lo")):
        for pos, oc in enumerate((4, 5, 6, 7, 0, 1, 2, 3)):
            sl = (8*phase + pos) % 3
            mb = [f"pql{sl}a", f"pql{sl}b"]
            op = sdma(sch, mb, t["pql"][:, sl, :, :],
                 r128(scr[mkey])[:, :, 128*oc:128*oc+128], name=f"p4_ldm{tc}_{phase}_{oc}")
            hp = "hi" if phase == 0 else "lo"
            sch.ops[op].deps = sorted(set(sch.ops[op].deps)
                | {sch.last_writer[f"hbm_m_{hp}_{r}_{h}"] for r in range(8) for h in range(2)})
            pb = f"pb{oc}"
            def ffn_mm(te, oc=oc, pbn=pb, phase=phase, sl=sl):
                ins = None
                for k in range(8):
                    ins = te.matmul(t[pbn][:], t["pql"][:, sl, k, :], t["bfB"][:, k, :],
                                    start=(phase == 0 and k == 0), stop=False)
                if phase == 1:
                    ins = te.matmul(t[pbn][:], t["fbrow"][0:1, 128*oc:128*oc+128],
                                    t["onesrow"][:], start=False, stop=True)
                return ins
            A("pe", ffn_mm, reads=["bfB", "fbrow", "onesrow"] + mb + ([pb] if phase else []),
              writes=[pb], name=f"p4_ffn{tc}_{phase}_{oc}")
    for oc in (4, 5, 6, 7, 0, 1, 2, 3):
        pb = f"pb{oc}"
        A("dve", lambda vE, oc=oc, pbn=pb, xc=xc:
            vE.scalar_tensor_tensor(xc[:, oc, :], t[pbn][:], 0.5, xc[:, oc, :], ALU.mult, ALU.add),
          reads=[pb, xcb], writes=[xcb], name=f"p4_x3_{tc}_{oc}")
    st_sdma(sch, [xcb], r128(yout)[:, :, 512*tc:512*tc+512], xc[:], name=f"p4_out{tc}")
    sch.last_writer[f"yout{tc}"] = sch.ops[-1].idx


def emit_debug(sch, t, v, scr, dbg, dbgset):
    if not dbgset:
        return
    for nm in ("x1v", "x1raw", "u1", "xln"):
        if nm in dbgset:
            st_sdma(sch, [], dbg[nm], scr[nm], name=f"dbg_{nm}")
    for nm, key in (("pthi", "pt_hi"), ("ptlo", "pt_lo"), ("qthi", "qt_hi"),
                    ("wrhi", "wrhi"), ("wrlo", "wrlo"), ("mlo", "m_lo")):
        if nm in dbgset:
            st_sdma(sch, [], dbg[nm], scr[key], name=f"dbg_{nm}")
    if "mmat" in dbgset:
        op = st_sdma(sch, [], dbg["mmat"], scr["m_hi"], name="dbg_mmat")
        sch.ops[op].deps = sorted(set(sch.ops[op].deps)
                                  | {sch.last_writer[f"hbm_m_hi_{r}_{h}"] for r in range(8) for h in range(2)})
    if "xn" in dbgset:
        st_sdma(sch, [f"xn{s}" for s in range(32)],
                dbg["xn"].rearrange("(a p) h -> p a h", p=128), v["xn"], name="dbg_xn")
    if "kp" in dbgset:
        st_sdma(sch, ["kp0", "kp1", "pqs0", "pqs1"], dbg["kp"].rearrange("(a p) h -> p a h", p=128),
                t["kp"][:], name="dbg_kp")
    if "kfcr" in dbgset:
        st_sdma(sch, [f"kfcr{c}" for c in range(3)],
                dbg["kfcr"].rearrange("(a p) h -> p a h", p=128), t["kfcr"][:], name="dbg_kfcr")
    if "kfci" in dbgset:
        st_sdma(sch, [f"kfci{c}" for c in range(3)],
                dbg["kfci"].rearrange("(a p) h -> p a h", p=128), t["kfci"][:], name="dbg_kfci")
    if "srb" in dbgset:
        st_sdma(sch, [f"srb{n}" for n in range(4)],
                dbg["srb"].rearrange("(a p) h -> p a h", p=128), t["srb"][:], name="dbg_srb")
    if "sib" in dbgset:
        st_sdma(sch, [f"sib{n}" for n in range(4)],
                dbg["sib"].rearrange("(a p) h -> p a h", p=128), t["sib"][:], name="dbg_sib")
    if "cctr" in dbgset:
        st_sdma(sch, [f"cct{n}" for n in range(4)],
                dbg["cctr"].rearrange("(a p) h -> p a h", p=128), t["cctr"][:], name="dbg_cctr")
    if "nccti" in dbgset:
        st_sdma(sch, [f"cct{n}" for n in range(4)],
                dbg["nccti"].rearrange("(a p) h -> p a h", p=128), t["nccti"][:], name="dbg_nccti")
    if "dfm" in dbgset:
        st_sdma(sch, [f"dfm{q}_{c}" for q in range(4) for c in range(8)],
                dbg["dfm"].rearrange("(a p) tt -> p a tt", p=128), v["dfm"], name="dbg_dfm")


def make_inputs(inputs, core):
    cst = host_constants()
    idx = inputs["L0_attn_idx"]
    x0 = np.ascontiguousarray(
        idx[BPC*core:BPC*core+BPC].reshape(BPC*SEQ, D_MODEL).T)
    m = {
        "x0": x0,
        "wxg": inputs["L0_attn_Wxg"],
        "wdin": inputs["L0_attn_Wdin"],
        "wgate": inputs["L0_attn_Wgate"],
        "wout": inputs["L0_attn_Wout"],
        "wfr": inputs["L0_ffn_w_freq_real"],
        "wfi": inputs["L0_ffn_w_freq_imag"],
        "crt": np.ascontiguousarray(inputs["L0_attn_CR"].T),
        "cit": np.ascontiguousarray(inputs["L0_attn_CI"].T),
        "lr": np.ascontiguousarray(inputs["L0_attn_LR"].reshape(4, 128).T),
        "li": np.ascontiguousarray(inputs["L0_attn_LI"].reshape(4, 128).T),
        "dvec": inputs["L0_attn_D"].reshape(1, 512),
        "n1w": np.ascontiguousarray(inputs["L0_attn_n1_w"].reshape(8, 128).T),
        "n1b": np.ascontiguousarray(inputs["L0_attn_n1_b"].reshape(8, 128).T),
        "n2w": inputs["L0_attn_n2_w"].reshape(1, 512),
        "n2b": inputs["L0_attn_n2_b"].reshape(1, 512),
        "fbias": inputs["L0_ffn_bias"].reshape(1, D_MODEL),
        "bout": inputs["L0_attn_bout"].reshape(1, D_MODEL),
        "cmat": cst["cmat"], "smat": cst["smat"],
        "fwdc": cst["fwdc"], "fwds": cst["fwds"],
        "invc": cst["invc"], "invs": cst["invs"],
        "identf": cst["identf"],
        "jrow": cst["jrow"],
    }
    return {k: np.ascontiguousarray(val) for k, val in m.items()}


def kernel(**inputs):
    nc = build_program()
    in_maps = [make_inputs(inputs, c) for c in range(NCORES)]
    res = run_bass_kernel_spmd(nc, in_maps, core_ids=list(range(NCORES)))
    outs = []
    for c in range(NCORES):
        y = np.asarray(res.results[c]["y"])
        outs.append(np.ascontiguousarray(y.T).reshape(BPC, SEQ, D_MODEL))
    return np.concatenate(outs, 0)


# version 10
# speedup vs baseline: 1.0201x; 1.0201x over previous
import numpy as np
import ml_dtypes
from contextlib import ExitStack

import concourse.bass as bass
import concourse.mybir as mybir
from concourse.bass_utils import run_bass_kernel_spmd

F32 = mybir.dt.float32
BF16 = mybir.dt.bfloat16
I32 = mybir.dt.int32
AF = mybir.ActivationFunctionType
ALU = mybir.AluOpType
AX = mybir.AxisListType
BFNP = ml_dtypes.bfloat16

D_MODEL = 1024
HID = 4096
SEQ = 2048
NB = 16
NCORES = 8
BPC = NB // NCORES
RT = BPC * SEQ
NTCH = RT // 512
H = 512
CC = 256
NCH = SEQ // CC
EPS = 1e-5
INV2PI = float(1.0/(2*np.pi))
MAGIC = 12582912.0
TWO_PI = float(2*np.pi)

ENGINES = ("pe", "act", "dve", "pool", "sp")
_SEC = {"pe": "tensor", "act": "scalar", "dve": "vector", "pool": "gpsimd", "sp": "sync"}


class Op:
    __slots__ = ("idx", "eng", "fn", "deps", "inc_amt", "cum", "name", "semkey")

    def __init__(self, idx, eng, fn, deps, inc_amt, name):
        self.idx = idx; self.eng = eng; self.fn = fn
        self.deps = deps; self.inc_amt = inc_amt
        self.cum = None; self.name = name


class Sched:
    def __init__(self, nc):
        self.nc = nc
        self.ops = []
        self.per_eng = {e: [] for e in ENGINES}
        self.last_writer = {}
        self.readers = {}
        self.needed = set()

    ALIAS = {"tb1": "mtB", "tb2": "rtB", "tbm": "stB", "tbs1": "fsB"}

    def add(self, eng, fn, reads=(), writes=(), deps=(), dma=False, name=""):
        reads = [self.ALIAS.get(b, b) for b in reads]
        writes = [self.ALIAS.get(b, b) for b in writes]
        dep_set = set(deps)
        for b in reads:
            if b in self.last_writer:
                dep_set.add(self.last_writer[b])
        for b in writes:
            if b in self.last_writer:
                dep_set.add(self.last_writer[b])
            dep_set.update(self.readers.get(b, ()))
        dep_set.discard(len(self.ops))
        op = Op(len(self.ops), eng, fn, sorted(dep_set), 16 if dma else 1, name)
        self.ops.append(op)
        self.per_eng[eng].append(op)
        for b in writes:
            self.last_writer[b] = op.idx
            self.readers[b] = set()
        for b in reads:
            self.readers.setdefault(b, set()).add(op.idx)
        return op.idx

    def emit(self, es, block):
        NDSEM = 12
        for op in self.ops:
            if op.inc_amt == 16:
                self.needed.add(op.idx)
            for d in op.deps:
                dep = self.ops[d]
                if dep.eng != op.eng or dep.inc_amt == 16 or dep.eng in ("dve", "act", "pool"):
                    self.needed.add(d)
        dma_ord = {e: 0 for e in ENGINES}
        cum = {}
        prev_slot_op = {}
        for op in self.ops:
            if op.inc_amt == 16:
                slot = dma_ord[op.eng] % NDSEM
                dma_ord[op.eng] += 1
                key = ("D", op.eng, slot)
                op.semkey = key
                cum[key] = cum.get(key, 0) + 16
                op.cum = cum[key]
                pk = (op.eng, slot)
                if pk in prev_slot_op:
                    op.deps = sorted(set(op.deps) | {prev_slot_op[pk]})
                prev_slot_op[pk] = op.idx
            elif op.idx in self.needed:
                key = ("C", op.eng)
                op.semkey = key
                cum[key] = cum.get(key, 0) + 1
                op.cum = cum[key]
            else:
                op.semkey = None
        sems = {}
        for key in cum:
            name = "_".join(str(x) for x in key)
            sems[key] = es.enter_context(self.nc.semaphore(f"S{name}"))
        for eng in ENGINES:
            ops = self.per_eng[eng]
            if not ops:
                continue
            sec_dec = getattr(block, _SEC[eng])

            def emit_engine(sec, ops=ops, eng=eng):
                hwm = {}
                for op in ops:
                    for d in op.deps:
                        dep = self.ops[d]
                        if dep.semkey is None:
                            assert dep.eng == eng, f"dep {dep.name} of {op.name} lacks sem"
                            continue
                        if dep.eng == eng and dep.inc_amt != 16 and eng == "pe":
                            continue
                        if hwm.get(dep.semkey, 0) < dep.cum:
                            sec.wait_ge(sems[dep.semkey], dep.cum)
                            hwm[dep.semkey] = dep.cum
                    last = op.fn(sec)
                    if op.semkey is not None and (op.idx in self.needed):
                        assert last is not None, f"op {op.name} returned no instruction"
                        last.then_inc(sems[op.semkey], op.inc_amt)
            sec_dec(emit_engine)


_CONST_CACHE = {}

def host_constants():
    if _CONST_CACHE:
        return _CONST_CACHE
    d = np.arange(D_MODEL)
    ang = 2*np.pi*np.outer(d, d)/D_MODEL
    _CONST_CACHE["cmat"] = np.cos(ang).astype(BFNP)
    _CONST_CACHE["smat"] = np.sin(ang).astype(BFNP)
    F2 = 2*CC
    m = np.arange(CC); f = np.arange(CC+1)
    a2 = 2*np.pi*np.outer(m, f)/F2
    fwdc = np.zeros((CC, 384), np.float32); fwds = np.zeros((CC, 384), np.float32)
    fwdc[:, :CC+1] = np.cos(a2)
    fwds[:, :CC+1] = -np.sin(a2)
    _CONST_CACHE["fwdc"] = fwdc.astype(BFNP)
    _CONST_CACHE["fwds"] = fwds.astype(BFNP)
    cf = np.where((f == 0) | (f == CC), 1.0, 2.0)
    ai = 2*np.pi*np.outer(f, np.arange(CC))/F2
    invc = np.zeros((384, CC), np.float32); invs = np.zeros((384, CC), np.float32)
    invc[:CC+1, :] = cf[:, None]*np.cos(ai)/F2
    invs[:CC+1, :] = -cf[:, None]*np.sin(ai)/F2
    _CONST_CACHE["invc"] = invc.astype(BFNP)
    _CONST_CACHE["invs"] = invs.astype(BFNP)
    _CONST_CACHE["identb"] = np.eye(128, dtype=BFNP)
    _CONST_CACHE["identf"] = np.eye(128, dtype=np.float32)
    _CONST_CACHE["jrow"] = np.tile(np.arange(257, dtype=np.float32), (128, 1))
    return _CONST_CACHE


def build_program(debug=()):
    nc = bass.Bass("TRN2", target_bir_lowering=False)
    dram = {}
    def din(name, shape, dt=F32):
        dram[name] = nc.dram_tensor(name, shape, dt, kind="ExternalInput").ap()
    din("x0", [D_MODEL, RT])
    din("wxg", [D_MODEL, HID])
    din("wdin", [D_MODEL, H])
    din("wgate", [H, HID])
    din("wout", [HID, D_MODEL])
    din("wfr", [D_MODEL, D_MODEL]); din("wfi", [D_MODEL, D_MODEL])
    din("crt", [512, 512]); din("cit", [512, 512])
    din("lr", [128, 4]); din("li", [128, 4])
    din("dvec", [1, 512])
    din("n1w", [128, 8]); din("n1b", [128, 8])
    din("n2w", [1, 512]); din("n2b", [1, 512])
    din("fbias", [1, D_MODEL]); din("bout", [1, D_MODEL])
    din("cmat", [D_MODEL, D_MODEL], BF16); din("smat", [D_MODEL, D_MODEL], BF16)
    din("fwdc", [CC, 384], BF16); din("fwds", [CC, 384], BF16)
    din("invc", [384, CC], BF16); din("invs", [384, CC], BF16)
    din("identf", [128, 128])
    din("jrow", [128, 257])
    yout = nc.dram_tensor("y", [D_MODEL, RT], F32, kind="ExternalOutput").ap()
    scr = {}
    scr["u1"] = nc.dram_tensor("scr_u1", [D_MODEL, RT], BF16, kind="Internal").ap()
    scr["x1v"] = nc.dram_tensor("scr_x1v", [D_MODEL, RT], F32, kind="Internal").ap()
    scr["xln"] = nc.dram_tensor("scr_xln", [D_MODEL, RT], BF16, kind="Internal").ap()
    scr["x1raw"] = nc.dram_tensor("scr_x1raw", [D_MODEL, RT], F32, kind="Internal").ap()
    scr["wxgb"] = nc.dram_tensor("scr_wxgb", [D_MODEL, 4096], BF16, kind="Internal").ap()
    scr["wgtb"] = nc.dram_tensor("scr_wgtb", [512, 4096], BF16, kind="Internal").ap()
    scr["woub"] = nc.dram_tensor("scr_woub", [4096, D_MODEL], BF16, kind="Internal").ap()
    for nm in ("pt_hi", "pt_lo", "qt_hi", "qt_lo", "m_hi", "m_lo",
               "wrhi", "wrlo", "wihi", "wilo"):
        scr[nm] = nc.dram_tensor("scr_" + nm, [D_MODEL, D_MODEL], BF16, kind="Internal").ap()

    dbg = {}
    def dout(name, shape, dt):
        dbg[name] = nc.dram_tensor("dbg_" + name, shape, dt, kind="ExternalOutput").ap()
    shapes = {
        "mmat": ([D_MODEL, D_MODEL], BF16),
        "pthi": ([D_MODEL, D_MODEL], BF16),
        "ptlo": ([D_MODEL, D_MODEL], BF16),
        "qthi": ([D_MODEL, D_MODEL], BF16),
        "wrhi": ([D_MODEL, D_MODEL], BF16),
        "wrlo": ([D_MODEL, D_MODEL], BF16),
        "mlo": ([D_MODEL, D_MODEL], BF16),
        "x1v": ([D_MODEL, RT], F32),
        "x1raw": ([D_MODEL, RT], F32),
        "ffnps": ([128, 512], F32),
        "pqltap": ([128, 8*128], BF16),
        "bfbtap": ([128, 8*512], BF16),
        "u1": ([D_MODEL, RT], BF16),
        "xln": ([D_MODEL, RT], BF16),
        "xn": ([RT, H], BF16),
        "kp": ([CC, H], BF16),
        "kfcr": ([384, H], BF16),
        "kfci": ([384, H], BF16),
        "srb": ([4*128, 257], BF16),
        "sib": ([4*128, 257], BF16),
        "cctr": ([512, 512], BF16),
        "nccti": ([512, 512], BF16),
        "dfm": ([H, RT], BF16),
    }
    for name in debug:
        dout(name, *shapes[name])

    es = ExitStack()
    t = {}
    with es:
        def T(n, s, d):
            t[n] = es.enter_context(nc.sbuf_tensor(n, s, d))
        T("H1", [128, 16384], BF16)
        T("H2", [128, 16384], BF16)
        T("pqs", [128, 2, 512], BF16)
        T("pql", [128, 3, 8, 128], BF16)
        T("c_fwdc", [128, 2, 384], BF16)
        T("c_fwds", [128, 2, 384], BF16)
        T("c_invc", [128, 3, 256], BF16)
        T("c_invs", [128, 3, 256], BF16)
        T("c_identf", [128, 128], F32)
        T("c_jrow", [128, 257], F32)
        T("mtB", [128, 512], F32)
        T("rtB", [128, 512], F32)
        T("stB", [128, 512], F32)
        T("fsB", [128, 512], F32)
        T("srb", [128, 4, 257], BF16)
        T("sib", [128, 4, 257], BF16)
        T("etr", [128, 2, 4, 128], BF16)
        T("eti", [128, 2, 4, 128], BF16)
        T("cctr", [128, 4, 512], BF16)
        T("nccti", [128, 4, 512], BF16)
        T("kfcr", [128, 3, 512], BF16)
        T("kfci", [128, 3, 512], BF16)
        T("lamc", [128, 16], F32)
        T("mcol", [128, 16], F32)
        T("n2sc", [128, 16], F32)
        T("epsc", [128, 4], F32)
        T("n2wt", [128, 512], BF16)
        T("n2bt", [128, 512], BF16)
        T("n2wb", [1, 512], BF16)
        T("n2bb", [1, 512], BF16)
        T("onesrow", [1, 512], BF16)
        T("ones1", [1, 128], BF16)
        T("fbrow", [1, 1024], BF16)
        T("borow", [1, 1024], BF16)
        T("c_n1w", [128, 8], F32)
        T("c_n1b", [128, 8], F32)
        T("onesbf", [128, 128], BF16)
        T("c_wdin", [128, 8, 512], BF16)
        T("xcA", [128, 8, 512], F32)
        T("xcB", [128, 8, 512], F32)
        T("bfA", [128, 8, 512], BF16)
        T("bfB", [128, 8, 512], BF16)
        T("mt", [128, 512], F32)
        T("rt_", [128, 512], F32)
        T("st_", [128, 512], F32)
        T("fscr", [128, 512], F32)
        T("xfr", [128, 3, 512], BF16)
        T("yfi", [128, 3, 512], BF16)
        T("wxgA", [128, 8, 256], BF16)
        T("wxgB", [128, 8, 256], BF16)
        T("wgtA", [128, 4, 256], BF16)
        T("wgtB", [128, 4, 256], BF16)
        T("wouA", [128, 1, 1024], BF16)
        T("wouB", [128, 1, 1024], BF16)
        t["kp"] = t["pqs"]
        t["dbrow"] = None
        for i in range(8):
            t[f"pb{i}"] = es.enter_context(nc.psum_tensor(f"pb{i}", [128, 512], F32))

        v = {}
        v["cmat"] = t["H1"][:, 0:8192].rearrange("p (a b) -> p a b", a=8)
        v["smat"] = t["H1"][:, 8192:16384].rearrange("p (a b) -> p a b", a=8)
        v["xn"] = t["H1"][:].rearrange("p (a b) -> p a b", a=32)
        v["xgP"] = t["H1"][:].rearrange("p (a b) -> p a b", a=32)
        v["wrhi"] = t["H2"][:, 0:4096].rearrange("p (a b) -> p a b", a=4)
        v["wrlo"] = t["H2"][:, 4096:8192].rearrange("p (a b) -> p a b", a=4)
        v["wihi"] = t["H2"][:, 8192:12288].rearrange("p (a b) -> p a b", a=4)
        v["wilo"] = t["H2"][:, 12288:16384].rearrange("p (a b) -> p a b", a=4)
        v["dfm"] = t["H2"][:].rearrange("p (a b) -> p a b", a=4)
        v["psir"] = t["xcA"][:, 0:4, :]
        v["psii"] = t["xcA"][:, 4:8, :]
        v["str_"] = t["xcB"][:, 0:4, :]
        v["sti"] = t["xcB"][:, 4:8, :]
        v["strb"] = t["bfB"][:, 0:4, :]
        v["nstib"] = t["bfB"][:, 4:8, :]
        v["dtm"] = [t["mt"], t["rt_"]]
        v["phR"] = t["wxgA"][:].rearrange("p a b -> p (a b)")
        v["phI"] = t["wxgB"][:].rearrange("p a b -> p (a b)")
        v["xf1"] = t["wgtA"][:].rearrange("p a b -> p (a b)")
        t["tb1"] = t["mtB"][:, 0:257]
        t["tb2"] = t["rtB"][:, 0:257]
        t["tbm"] = t["stB"][:, 0:257]
        t["tbs1"] = t["fsB"][:, 0:257]

        for _e in ("dve", "act", "pool"):
            CHAIN[_e] = (es.enter_context(nc.semaphore(f"CH{_e}")), 0)
        sch = Sched(nc)
        block = es.enter_context(nc.Block())
        emit_all(sch, t, v, dram, scr, yout, dbg, set(debug))
        sch.emit(es, block)
    return nc


def sdma(sch, bufs, dst, src, name="ld"):
    if isinstance(bufs, str):
        bufs = [bufs]
    return sch.add("sp", lambda s, d=dst, r=src: s.dma_start(out=d, in_=r),
                   writes=bufs, dma=True, name=name)

def gdma(sch, bufs, dst, src, name="ldg"):
    if isinstance(bufs, str):
        bufs = [bufs]
    return sch.add("pool", lambda g, d=dst, r=src: g.dma_start(out=d, in_=r),
                   writes=bufs, dma=True, name=name)

def st_sdma(sch, reads, dst, src, name="st"):
    return sch.add("pool", lambda s, d=dst, r=src: s.dma_start(out=d, in_=r),
                   reads=reads, dma=True, name=name)

def r128(ap):
    return ap.rearrange("(a p) f -> p a f", p=128)


DBG_TAPS = {}
CHAIN = {}

def BAR(sec, eng, ins):
    sem, cnt = CHAIN[eng]
    cnt += 1
    CHAIN[eng] = (sem, cnt)
    ins.then_inc(sem, 1)
    sec.wait_ge(sem, cnt)

def emit_all(sch, t, v, dram, scr, yout, dbg, dbgset):
    DBG_TAPS.clear()
    DBG_TAPS.update(dbg)
    emit_p0_loads(sch, t, v, dram)
    conv_jobs = []

    def conv_w(nm, key, rows, cols, bufs, bnames, flat_cols):
        ch = []
        for r0 in range(0, rows, 128):
            for c0 in range(0, cols, flat_cols):
                ch.append((r0, c0))
        for ci, (r0, c0) in enumerate(ch):
            def job(ci=ci, r0=r0, c0=c0, nm=nm, key=key, bufs=bufs, bnames=bnames,
                    flat_cols=flat_cols):
                b = ci % 2
                ap = bufs[b][:, 0:flat_cols]
                gdma(sch, [bnames[b]], ap, dram[nm][r0:r0+128, c0:c0+flat_cols],
                     name=f"cwl_{nm}{ci}")
                op = st_sdma(sch, [bnames[b]], scr[key][r0:r0+128, c0:c0+flat_cols], ap,
                             name=f"cws_{nm}{ci}")
                sch.last_writer[f"cw_{nm}"] = op
            conv_jobs.append(job)
    wxgf = [t["wxgA"][:].rearrange("p a b -> p (a b)"), t["wxgB"][:].rearrange("p a b -> p (a b)")]
    wgtf = [t["wgtA"][:].rearrange("p a b -> p (a b)"), t["wgtB"][:].rearrange("p a b -> p (a b)")]
    conv_w("wxg", "wxgb", D_MODEL, 4096, wxgf, ["wxg0", "wxg1"], 2048)
    conv_w("wgate", "wgtb", 512, 4096, wgtf, ["wgt0", "wgt1"], 1024)
    conv_w("wout", "woub", 4096, D_MODEL, wgtf, ["wgt0", "wgt1"], 1024)
    emit_wprep(sch, t, v, dram, scr)
    mparts = list(mbuild_parts(sch, t, v, scr))
    per = (len(mparts) + NTCH - 1) // NTCH
    cper = (len(conv_jobs) + NTCH - 1) // NTCH
    for tc in range(NTCH):
        for _ in range(per):
            if mparts:
                mparts.pop(0)()
        emit_p1_pre(sch, t, v, dram, scr, tc)
        for _ in range(cper):
            if conv_jobs:
                conv_jobs.pop(0)()
    while mparts:
        mparts.pop(0)()
    while conv_jobs:
        conv_jobs.pop(0)()
    emit_tables(sch, t, v, dram)
    for tc in range(NTCH):
        emit_p1_post(sch, t, v, dram, scr, tc)
    for sq in range(BPC):
        emit_p2_seq(sch, t, v, sq)
    for tc in range(NTCH):
        emit_p3_chunk(sch, t, v, dram, scr, tc)
        emit_p4_chunk(sch, t, v, dram, scr, yout, tc)
    emit_debug(sch, t, v, scr, dbg, dbgset)
    sch.add("sp", lambda s: s.nop(), reads=[f"yout{c}" for c in range(NTCH)], name="terminal")


def emit_p0_loads(sch, t, v, dram):
    A = sch.add
    sdma(sch, "c_fwdc", t["c_fwdc"][:], r128(dram["fwdc"]))
    sdma(sch, "c_fwds", t["c_fwds"][:], r128(dram["fwds"]))
    sdma(sch, "c_invc", t["c_invc"][:], r128(dram["invc"]))
    sdma(sch, "c_invs", t["c_invs"][:], r128(dram["invs"]))
    sdma(sch, "c_identf", t["c_identf"][:], dram["identf"])
    sdma(sch, "c_jrow", t["c_jrow"][:], dram["jrow"])
    sdma(sch, "c_n1w", t["c_n1w"][:], dram["n1w"])
    sdma(sch, "c_n1b", t["c_n1b"][:], dram["n1b"])
    sdma(sch, ["cmat", "H1"], v["cmat"], r128(dram["cmat"]))
    sdma(sch, ["smat", "H1"], v["smat"], r128(dram["smat"]))
    sdma(sch, [f"md_lr{n}" for n in range(4)], t["mcol"][:, 0:4], dram["lr"], name="ld_lr")
    sdma(sch, [f"md_li{n}" for n in range(4)], t["mcol"][:, 4:8], dram["li"], name="ld_li")
    gdma(sch, "fbrow", t["fbrow"][:], dram["fbias"])
    gdma(sch, "borow", t["borow"][:], dram["bout"])
    gdma(sch, "cctr_raw", t["cctr"][:], r128(dram["crt"]))
    gdma(sch, "ccti_raw", t["nccti"][:], r128(dram["cit"]))
    gdma(sch, "c_wdin", t["c_wdin"][:], r128(dram["wdin"]))
    gdma(sch, "n2wb", t["n2wb"][:], dram["n2w"])
    gdma(sch, "n2bb", t["n2bb"][:], dram["n2b"])
    gdma(sch, "dbrow", t["wouA"][0:1, 0, 0:512], dram["dvec"])
    def ms(buf, ap, val):
        A("dve", lambda vE, a=ap, q=val: vE.memset(a, q), writes=[buf], name=f"ms_{buf}")
    ms("onesbf", t["onesbf"][:], 1.0/1024.)
    ms("onesrow", t["onesrow"][:], 1.0)
    ms("ones1", t["ones1"][:], 1.0)
    A("dve", lambda vE: vE.memset(t["epsc"][:, 0:1], EPS), writes=["epsc0"], name="ms_eps")
    A("dve", lambda vE: vE.memset(t["epsc"][:, 1:2], EPS*EPS), writes=["epsc1"], name="ms_eps2")


def emit_wprep(sch, t, v, dram, scr):
    A = sch.add
    for wi_, wname in enumerate(("wfr", "wfi")):
        for kc in range(8):
            N, S = sset(t, kc % 2)
            src32a = dram[wname][128*kc:128*kc+128, 0:512]
            src32b = dram[wname][128*kc:128*kc+128, 512:1024]
            sdma(sch, N["mt"], S["mt"][:], src32a, name=f"wp_l{wi_}{kc}a")
            sdma(sch, N["rt"], S["rt"][:], src32b, name=f"wp_l{wi_}{kc}b")
            if kc % 2 == 0:
                hi, hin = t["bfA"][:, 0:2, :], ["bfA"]
                lo, lon = t["bfB"][:, 0:2, :], ["bfB"]
            else:
                hi, hin = t["pqs"][:, 0:2, :], ["pqs0", "pqs1"]
                lo, lon = t["xfr"][:, 0:2, :], ["xfr0", "xfr1"]
            def hifn(vE, hi=hi, S=S):
                vE.tensor_copy(hi[:, 0, :], S["mt"][:])
                return vE.tensor_copy(hi[:, 1, :], S["rt"][:])
            A("dve", hifn, reads=[N["mt"], N["rt"]] + hin, writes=hin, name=f"wp_hi{wi_}{kc}")
            def lofn(vE, hi=hi, lo=lo, S=S):
                vE.tensor_tensor(lo[:, 0, :], S["mt"][:], hi[:, 0, :], ALU.subtract)
                return vE.tensor_tensor(lo[:, 1, :], S["rt"][:], hi[:, 1, :], ALU.subtract)
            A("dve", lofn, reads=[N["mt"], N["rt"]] + hin + lon, writes=lon, name=f"wp_lo{wi_}{kc}")
            hname = ("wrhi", "wihi")[wi_]; lname = ("wrlo", "wilo")[wi_]
            op = st_sdma(sch, hin, scr[hname][128*kc:128*kc+128, :],
                    hi.rearrange("p a b -> p (a b)"), name=f"wp_sh{wi_}{kc}")
            sch.last_writer[f"hw_{hname}{kc}"] = op
            op = st_sdma(sch, lon, scr[lname][128*kc:128*kc+128, :],
                    lo.rearrange("p a b -> p (a b)"), name=f"wp_sl{wi_}{kc}")
            sch.last_writer[f"hw_{lname}{kc}"] = op


def mbuild_parts(sch, t, v, scr):
    A = sch.add
    wv = {"wrhi": v["wrhi"], "wrlo": v["wrlo"], "wihi": v["wihi"], "wilo": v["wilo"]}

    res = {}

    def load_k(k, tag):
        sl = k % 4
        if res.get(sl) == k:
            return
        res[sl] = k
        for nm in ("wrhi", "wrlo", "wihi", "wilo"):
            op = gdma(sch, [f"{nm}s{sl}"], wv[nm][:, sl:sl+1, :],
                 r128(scr[nm])[:, k:k+1, :], name=f"mb_{tag}_{nm}{k}")
            sch.ops[op].deps = sorted(set(sch.ops[op].deps)
                | {sch.last_writer[f"hw_{nm}{k}"]})

    def split_evac(pb, dsthi, dstlo, ccv, rhv, scale, tag, hkey):
        A("act", lambda sc, pbn=pb: sc.mul(t["pqs"][:, 0, :], t[pbn][:], scale),
          reads=[pb, "pqs0"], writes=["pqs0"], name=f"{tag}_evh")
        A("dve", lambda vE, pbn=pb: vE.scalar_tensor_tensor(
            t["xfr"][:, 0, :], t[pbn][:], scale, t["pqs"][:, 0, :], ALU.mult, ALU.subtract),
          reads=[pb, "pqs0", "xfr0"], writes=["xfr0"], name=f"{tag}_evl")
        op = st_sdma(sch, ["pqs0"], dsthi.rearrange("(a p) r -> p a r", p=128)[:, ccv, 512*rhv:512*rhv+512],
                t["pqs"][:, 0, :], name=f"{tag}_sth")
        sch.last_writer[f"hbm_{hkey}_hi_{ccv}_{rhv}"] = op
        op = st_sdma(sch, ["xfr0"], dstlo.rearrange("(a p) r -> p a r", p=128)[:, ccv, 512*rhv:512*rhv+512],
                t["xfr"][:, 0, :], name=f"{tag}_stl")
        sch.last_writer[f"hbm_{hkey}_lo_{ccv}_{rhv}"] = op

    def stage1_batch(groups, kind, tag, korder):
        for kpos, k in enumerate(korder):
            load_k(k, tag)
            sl = k % 4
            first = kpos == 0
            last = kpos == 7
            slr = [f"{nm}s{sl}" for nm in ("wrhi", "wrlo", "wihi", "wilo")]
            for bi, (cc_, rh) in enumerate(groups):
                if kind == "pt":
                    pb = f"pb{bi}"
                    def mmfn(te, cc_=cc_, rh=rh, pbn=pb, sl=sl, k=k, first=first, last=last):
                        ins = None
                        for wn, cn in (("wrhi", "cmat"), ("wrlo", "cmat"),
                                       ("wihi", "smat"), ("wilo", "smat")):
                            ins = te.matmul(t[pbn][:], wv[wn][:, sl, 128*cc_:128*cc_+128],
                                            v[cn][:, k, 512*rh:512*rh+512],
                                            start=(first and wn == "wrhi"),
                                            stop=(last and wn == "wilo"))
                        return ins
                    A("pe", mmfn, reads=slr + ["cmat", "smat", "H2", pb],
                      writes=[pb], name=f"{tag}_mm{k}_{bi}")
                else:
                    pba, pbb = f"pb{2*bi}", f"pb{2*bi+1}"
                    def mmfa(te, cc_=cc_, rh=rh, pbn=pba, sl=sl, k=k, first=first, last=last):
                        ins = None
                        for wn in ("wihi", "wilo"):
                            ins = te.matmul(t[pbn][:], wv[wn][:, sl, 128*cc_:128*cc_+128],
                                            v["cmat"][:, k, 512*rh:512*rh+512],
                                            start=(first and wn == "wihi"),
                                            stop=(last and wn == "wilo"))
                        return ins
                    def mmfb(te, cc_=cc_, rh=rh, pbn=pbb, sl=sl, k=k, first=first, last=last):
                        ins = None
                        for wn in ("wrhi", "wrlo"):
                            ins = te.matmul(t[pbn][:], wv[wn][:, sl, 128*cc_:128*cc_+128],
                                            v["smat"][:, k, 512*rh:512*rh+512],
                                            start=(first and wn == "wrhi"),
                                            stop=(last and wn == "wrlo"))
                        return ins
                    A("pe", mmfa, reads=slr + ["cmat", "H2", pba], writes=[pba],
                      name=f"{tag}_mma{k}_{bi}")
                    A("pe", mmfb, reads=slr + ["smat", "H2", pbb], writes=[pbb],
                      name=f"{tag}_mmb{k}_{bi}")
        for bi, (cc_, rh) in enumerate(groups):
            if kind == "pt":
                split_evac(f"pb{bi}", scr["pt_hi"], scr["pt_lo"], cc_, rh, 1.0,
                           f"{tag}_e{bi}", "pt")
            else:
                pba, pbb = f"pb{2*bi}", f"pb{2*bi+1}"
                A("act", lambda sc, pbn=pbb: sc.copy(t["fscr"][:], t[pbn][:]),
                  reads=[pbb, "fscr"], writes=["fscr"], name=f"{tag}_cp{bi}")
                A("dve", lambda vE, pbn=pba:
                    vE.tensor_tensor(t["fscr"][:], t["fscr"][:], t[pbn][:], ALU.subtract),
                  reads=[pba, "fscr"], writes=["fscr"], name=f"{tag}_sub{bi}")
                A("act", lambda sc: sc.copy(t["pqs"][:, 0, :], t["fscr"][:]),
                  reads=["fscr", "pqs0"], writes=["pqs0"], name=f"{tag}_hi{bi}")
                A("dve", lambda vE:
                    vE.tensor_tensor(t["xfr"][:, 0, :], t["fscr"][:], t["pqs"][:, 0, :], ALU.subtract),
                  reads=["fscr", "pqs0", "xfr0"], writes=["xfr0"], name=f"{tag}_lo{bi}")
                op = st_sdma(sch, ["pqs0"],
                        scr["qt_hi"].rearrange("(a p) r -> p a r", p=128)[:, cc_, 512*rh:512*rh+512],
                        t["pqs"][:, 0, :], name=f"{tag}_sth{bi}")
                sch.last_writer[f"hbm_qt_hi_{cc_}_{rh}"] = op
                op = st_sdma(sch, ["xfr0"],
                        scr["qt_lo"].rearrange("(a p) r -> p a r", p=128)[:, cc_, 512*rh:512*rh+512],
                        t["xfr"][:, 0, :], name=f"{tag}_stl{bi}")
                sch.last_writer[f"hbm_qt_lo_{cc_}_{rh}"] = op
        return

    groups = [(cc, rh) for cc in range(8) for rh in range(2)]
    korders = [list(range(8)), list(range(7, -1, -1))]
    for b in range(2):
        yield (lambda b=b: stage1_batch(groups[8*b:8*b+8], "pt", f"pt{b}", korders[b % 2]))
    for b in range(4):
        yield (lambda b=b: stage1_batch(groups[4*b:4*b+4], "qt", f"qt{b}", korders[b % 2]))
    def m_part(r):
        for rnd, (ph, qh) in enumerate((("pt_hi", "qt_hi"), ("pt_lo", "qt_lo"))):
            hp = "hi" if rnd == 0 else "lo"
            for hf in range(2):
                cr = list(range(4*hf, 4*hf+4))
                op = sdma(sch, [f"pql0{'ab'[hf]}"], t["pql"][:, 0, 4*hf:4*hf+4, :],
                     r128(scr[ph])[:, 4*hf:4*hf+4, 128*r:128*r+128], name=f"m_ldp{r}_{rnd}{hf}")
                sch.ops[op].deps = sorted(set(sch.ops[op].deps)
                    | {sch.last_writer[f"hbm_pt_{hp}_{cc}_{rh}"] for cc in cr for rh in range(2)})
                op = sdma(sch, [f"pql1{'ab'[hf]}"], t["pql"][:, 1, 4*hf:4*hf+4, :],
                     r128(scr[qh])[:, 4*hf:4*hf+4, 128*r:128*r+128], name=f"m_ldq{r}_{rnd}{hf}")
                sch.ops[op].deps = sorted(set(sch.ops[op].deps)
                    | {sch.last_writer[f"hbm_qt_{hp}_{cc}_{rh}"] for cc in cr for rh in range(2)})
                for oh in range(2):
                    pb = f"pb{(r*2+oh) % 8}"
                    def mfn(te, r=r, oh=oh, pbn=pb, rnd=rnd, hf=hf):
                        ins = None
                        for cc2 in range(4*hf, 4*hf+4):
                            te.matmul(t[pbn][:], t["pql"][:, 0, cc2, :],
                                      v["cmat"][:, cc2, 512*oh:512*oh+512],
                                      start=(rnd == 0 and cc2 == 0), stop=False)
                            ins = te.matmul(t[pbn][:], t["pql"][:, 1, cc2, :],
                                            v["smat"][:, cc2, 512*oh:512*oh+512],
                                            start=False, stop=(rnd == 1 and cc2 == 7))
                        return ins
                    A("pe", mfn, reads=[f"pql0{'ab'[hf]}", f"pql1{'ab'[hf]}", "cmat", "smat", pb],
                      writes=[pb], name=f"m_mm{r}_{oh}_{rnd}{hf}")
        for oh in range(2):
            pb = f"pb{(r*2+oh) % 8}"
            split_evac(pb, scr["m_hi"], scr["m_lo"], r, oh, 1.0/1024., f"m_e{r}_{oh}", "m")
    for r in range(8):
        yield (lambda r=r: m_part(r))


def emit_tables(sch, t, v, dram):
    A = sch.add
    jr = t["c_jrow"][:]
    for n in range(4):
        a_ap = t["lamc"][:, 12+n:13+n]
        b_ap = t["mcol"][:, 8+n:9+n]
        A("act", lambda sc, o=a_ap, i=t["mcol"][:, n:n+1]: sc.activation(o, i, AF.Exp),
          reads=[f"md_lr{n}"], writes=[f"a{n}"], name=f"tb_a{n}")
        A("act", lambda sc, o=b_ap, i=t["mcol"][:, 4+n:5+n]: sc.activation(o, i, AF.Exp),
          reads=[f"md_li{n}"], writes=[f"b{n}"], name=f"tb_b{n}")
        c_ap = t["mcol"][:, 12+n:13+n]
        chi = t["n2sc"][:, 8+n:9+n]
        clo = t["n2sc"][:, 12+n:13+n]
        def csplit(vE, b=b_ap, c=c_ap, chi=chi, clo=clo):
            BAR(vE, "dve", vE.tensor_scalar_mul(c, b, INV2PI))
            BAR(vE, "dve", vE.tensor_scalar(chi, c, 4096.0, MAGIC, ALU.mult, ALU.add))
            BAR(vE, "dve", vE.tensor_scalar_add(chi, chi, -MAGIC))
            BAR(vE, "dve", vE.tensor_scalar_mul(chi, chi, 1.0/4096.0))
            return vE.tensor_tensor(clo, c, chi, ALU.subtract)
        A("dve", csplit, reads=[f"b{n}"], writes=[f"chi{n}"], name=f"tb_csplit{n}")
        def fracA(vE, chi=chi, clo=clo):
            BAR(vE, "dve", vE.tensor_scalar(t["tb1"][:], jr, chi, None, ALU.mult))
            BAR(vE, "dve", vE.tensor_scalar_add(t["tb2"][:], t["tb1"][:], MAGIC))
            BAR(vE, "dve", vE.tensor_scalar_add(t["tb2"][:], t["tb2"][:], -MAGIC))
            BAR(vE, "dve", vE.tensor_tensor(t["tb1"][:], t["tb1"][:], t["tb2"][:], ALU.subtract))
            BAR(vE, "dve", vE.tensor_scalar(t["tb2"][:], jr, clo, None, ALU.mult))
            BAR(vE, "dve", vE.tensor_tensor(t["tb1"][:], t["tb1"][:], t["tb2"][:], ALU.add))
            BAR(vE, "dve", vE.tensor_scalar_add(t["tb2"][:], t["tb1"][:], MAGIC))
            BAR(vE, "dve", vE.tensor_scalar_add(t["tb2"][:], t["tb2"][:], -MAGIC))
            return vE.tensor_tensor(t["tb2"][:], t["tb1"][:], t["tb2"][:], ALU.subtract)
        A("dve", fracA, reads=[f"chi{n}", "c_jrow", "tb1", "tb2"],
          writes=["tb1", "tb2"], name=f"tb_fracA{n}")
        A("act", lambda sc: sc.activation(t["tbs1"][:], t["tb2"][:], AF.Sin, scale=TWO_PI),
          reads=["tb2", "tbs1"], writes=["tbs1"], name=f"tb_sin{n}")
        def fracB(vE):
            BAR(vE, "dve", vE.tensor_scalar_add(t["tb1"][:], t["tb1"][:], 0.25))
            BAR(vE, "dve", vE.tensor_scalar_add(t["tb2"][:], t["tb1"][:], MAGIC))
            BAR(vE, "dve", vE.tensor_scalar_add(t["tb2"][:], t["tb2"][:], -MAGIC))
            return vE.tensor_tensor(t["tb1"][:], t["tb1"][:], t["tb2"][:], ALU.subtract)
        A("dve", fracB, reads=["tb1", "tb2"], writes=["tb1", "tb2"],
          name=f"tb_fracB{n}")
        A("act", lambda sc: sc.activation(t["tb2"][:], t["tb1"][:], AF.Sin, scale=TWO_PI),
          reads=["tb1", "tb2"], writes=["tb2"], name=f"tb_cos{n}")
        na = t["n2sc"][:, 8+n:9+n]
        A("dve", lambda vE, na=na, a=a_ap: vE.tensor_scalar_mul(na, a, -1.0),
          reads=[f"a{n}", f"chi{n}"], writes=[f"na{n}"], name=f"tb_na{n}")
        A("act", lambda sc, na=na: sc.activation(t["tbm"][:], jr, AF.Exp, scale=na),
          reads=[f"na{n}", "c_jrow", "tbm"], writes=["tbm"], name=f"tb_mag{n}")
        A("dve", lambda vE: vE.tensor_tensor(t["tb1"][:], t["tbm"][:], t["tb2"][:], ALU.mult),
          reads=["tbm", "tb2", "tb1"], writes=["tb1"], name=f"tb_srf{n}")
        A("dve", lambda vE: vE.tensor_tensor(t["tbs1"][:], t["tbm"][:], t["tbs1"][:], ALU.mult),
          reads=["tbm", "tbs1"], writes=["tbs1"], name=f"tb_sif{n}")
        A("dve", lambda vE, n=n: vE.tensor_copy(t["srb"][:, n, :], t["tb1"][:]),
          reads=["tb1"], writes=[f"srb{n}"], name=f"tb_srb{n}")
        A("dve", lambda vE, n=n: vE.tensor_copy(t["sib"][:, n, :], t["tbs1"][:]),
          reads=["tbs1"], writes=[f"sib{n}"], name=f"tb_sib{n}")
        def lamcols(vE, n=n):
            vE.tensor_copy(t["lamc"][:, n:n+1], t["tb1"][:, 256:257])
            vE.tensor_copy(t["lamc"][:, 4+n:5+n], t["tbs1"][:, 256:257])
            return vE.tensor_scalar_mul(t["lamc"][:, 8+n:9+n], t["tbs1"][:, 256:257], -1.0)
        A("dve", lamcols, reads=["tb1", "tbs1"], writes=[f"lamc{n}"], name=f"tb_lamc{n}")
        for mch in range(2):
            hi = CC - 128*mch
            for im in range(2):
                srcb = "tb1" if im == 0 else "tbs1"
                dst = "etr" if im == 0 else "eti"
                A("dve", lambda vE, hi=hi, srcb=srcb:
                    vE.tensor_copy(t["fscr"][:, 0:128], t[srcb][:, hi-128:hi][:, ::-1]),
                  reads=[srcb, "fscr"], writes=["fscr"], name=f"et_rv{n}{mch}{im}")
                def trf(te):
                    return te.transpose(t["pb5"][:, 0:128], t["fscr"][:, 0:128], t["c_identf"][:])
                A("pe", trf, reads=["fscr", "c_identf"], writes=["pb5"], name=f"et_tr{n}{mch}{im}")
                def evf(sc, n=n, mch=mch, dst=dst):
                    return sc.copy(t[dst][:, mch, n, :], t["pb5"][:, 0:128])
                A("act", evf, reads=["pb5"], writes=[f"{dst}{n}{mch}"], name=f"et_ev{n}{mch}{im}")
        def wpipe(vE, n=n):
            ur = t["n2sc"][:, 12+n:13+n]
            ui = t["mcol"][:, 12+n:13+n]
            vE.tensor_scalar_add(ur, t["tb1"][:, 1:2], -1.0)
            BAR(vE, "dve", vE.tensor_copy(ui, t["tbs1"][:, 1:2]))
            a = t["lamc"][:, 12+n:13+n]
            b = t["mcol"][:, 8+n:9+n]
            dn = t["tb2"][:, 0:1]
            vE.tensor_tensor(dn, a, a, ALU.mult)
            BAR(vE, "dve", vE.tensor_tensor(t["tb2"][:, 1:2], b, b, ALU.mult))
            BAR(vE, "dve", vE.tensor_tensor(dn, dn, t["tb2"][:, 1:2], ALU.add))
            BAR(vE, "dve", vE.reciprocal(dn, dn))
            wr = t["tb2"][:, 2:3]; s_ = t["tb2"][:, 3:4]
            vE.tensor_tensor(wr, ui, b, ALU.mult)
            BAR(vE, "dve", vE.tensor_tensor(t["tb2"][:, 4:5], ur, a, ALU.mult))
            BAR(vE, "dve", vE.tensor_tensor(wr, wr, t["tb2"][:, 4:5], ALU.subtract))
            BAR(vE, "dve", vE.tensor_tensor(wr, wr, dn, ALU.mult))
            vE.tensor_tensor(s_, ui, a, ALU.mult)
            BAR(vE, "dve", vE.tensor_tensor(t["tb2"][:, 4:5], ur, b, ALU.mult))
            BAR(vE, "dve", vE.tensor_tensor(s_, s_, t["tb2"][:, 4:5], ALU.add))
            BAR(vE, "dve", vE.tensor_tensor(s_, s_, dn, ALU.mult))
            c_raw = t["cctr"][:, n, :]; i_raw = t["nccti"][:, n, :]
            t1 = t["kfcr"][:, 0, :]; t2 = t["kfcr"][:, 1, :]
            vE.tensor_scalar(t1, c_raw, wr, None, ALU.mult)
            BAR(vE, "dve", vE.tensor_scalar(t2, i_raw, s_, None, ALU.mult))
            vE.tensor_scalar(t["st_"][:], c_raw, s_, None, ALU.mult)
            BAR(vE, "dve", vE.tensor_scalar(t["fscr"][:], i_raw, wr, None, ALU.mult))
            BAR(vE, "dve", vE.tensor_tensor(t["cctr"][:, n, :], t1, t2, ALU.add))
            return vE.tensor_tensor(t["nccti"][:, n, :], t["st_"][:], t["fscr"][:], ALU.subtract)
        A("dve", wpipe,
          reads=["tb1", "tbs1", f"a{n}", f"b{n}", "cctr_raw", "ccti_raw", "tb2",
                 "kfcr0", "kfcr1", "st_", "fscr"],
          writes=[f"cct{n}", "kfcr0", "kfcr1", "st_", "fscr", "tb2"], name=f"tb_w{n}")
    emit_tables_k(sch, t)
    A("pe", lambda te: te.matmul(t["pb6"][:], t["ones1"][:], t["n2wb"][:], start=True, stop=True),
      reads=["ones1", "n2wb"], writes=["pb6"], name="n2w_mm")
    A("act", lambda sc: sc.copy(t["n2wt"][:], t["pb6"][:]), reads=["pb6"], writes=["n2wt"],
      name="n2w_ev")
    A("pe", lambda te: te.matmul(t["pb6"][:], t["ones1"][:], t["n2bb"][:], start=True, stop=True),
      reads=["ones1", "n2bb"], writes=["pb6"], name="n2b_mm")
    A("act", lambda sc: sc.copy(t["n2bt"][:], t["pb6"][:]), reads=["pb6"], writes=["n2bt"],
      name="n2b_ev")


def emit_tables_k(sch, t):
    A = sch.add
    for jc in range(2):
        pb = f"pb{jc}"
        def kmm(te, jc=jc, pbn=pb):
            ins = None
            for n in range(4):
                te.matmul(t[pbn][:], t["srb"][:, n, 128*jc:128*jc+128], t["cctr"][:, n, :],
                          start=(n == 0), stop=False)
                ins = te.matmul(t[pbn][:], t["sib"][:, n, 128*jc:128*jc+128], t["nccti"][:, n, :],
                                start=False, stop=(n == 3))
            return ins
        A("pe", kmm, reads=[f"srb{n}" for n in range(4)] + [f"sib{n}" for n in range(4)]
          + [f"cct{n}" for n in range(4)], writes=[pb], name=f"k_mm{jc}")
        A("act", lambda sc, jc=jc, pbn=pb: sc.copy(t["kp"][:, jc, :], t[pbn][:]),
          reads=[pb], writes=[f"kp{jc}", f"pqs{jc}"], name=f"k_ev{jc}")
    A("dve", lambda vE: vE.tensor_tensor(t["kp"][0:1, 0, :], t["kp"][0:1, 0, :],
                                         t["wouA"][0:1, 0, 0:512], ALU.add),
      reads=["kp0", "dbrow", "wou0"], writes=["kp0", "pqs0"], name="k_dtap")
    for fc in range(3):
        for im in range(2):
            mat = "c_fwdc" if im == 0 else "c_fwds"
            dst = "kfcr" if im == 0 else "kfci"
            pb = f"pb{(2 + fc*2 + im) % 8}"
            def kfmm(te, fc=fc, mat=mat, pbn=pb):
                ins = None
                for mch in range(2):
                    ins = te.matmul(t[pbn][:], t[mat][:, mch, 128*fc:128*fc+128],
                                    t["kp"][:, mch, :], start=(mch == 0), stop=(mch == 1))
                return ins
            A("pe", kfmm, reads=["kp0", "kp1", "pqs0", "pqs1", mat], writes=[pb],
              name=f"kf_mm{fc}{im}")
            wr_extra = [f"{dst}{fc}"]
            if dst == "kfcr" and fc in (0, 1):
                wr_extra.append(f"kfcr{fc}_scr")
            A("act", lambda sc, fc=fc, dst=dst, pbn=pb: sc.copy(t[dst][:, fc, :], t[pbn][:]),
              reads=[pb] + (["kfcr0", "kfcr1"] if dst == "kfcr" and fc < 2 else []),
              writes=wr_extra + ([f"{dst}{fc}"] if False else []), name=f"kf_ev{fc}{im}")


def sset(t, par):
    if par == 0:
        n = {"mt": "mt", "rt": "rt_", "st": "st_", "fs": "fscr"}
    else:
        n = {"mt": "mtB", "rt": "rtB", "st": "stB", "fs": "fsB"}
    return n, {k: t[v] for k, v in n.items()}


def emit_p1_pre(sch, t, v, dram, scr, tc):
    A = sch.add
    N, S = sset(t, tc % 2)
    xc = t["xcA"] if tc % 2 == 0 else t["xcB"]
    xcb = f"xc{tc % 2}"
    sdma(sch, xcb, xc[:], r128(dram["x0"])[:, :, 512*tc:512*tc+512], name=f"p1a_ld{tc}")
    A("dve", lambda vE, xc=xc: vE.tensor_copy(t["bfA"][:], xc[:]),
      reads=[xcb], writes=["bfA"], name=f"p1a_cast{tc}")
    A("act", lambda sc, xc=xc: sc.activation(t["bfB"][:], xc[:], AF.Square),
      reads=[xcb], writes=["bfB"], name=f"p1a_sq{tc}")
    def stats1(te):
        ins = None
        for k in range(8):
            ins = te.matmul(t["pb6"][:], t["onesbf"][:], t["bfA"][:, k, :],
                            start=(k == 0), stop=(k == 7))
        return ins
    A("pe", stats1, reads=["bfA", "onesbf"], writes=["pb6"], name=f"p1a_s1_{tc}")
    def stats2(te):
        ins = None
        for k in range(8):
            ins = te.matmul(t["pb7"][:], t["onesbf"][:], t["bfB"][:, k, :],
                            start=(k == 0), stop=(k == 7))
        return ins
    A("pe", stats2, reads=["bfB", "onesbf"], writes=["pb7"], name=f"p1a_s2_{tc}")
    A("act", lambda sc: sc.copy(S["mt"][:], t["pb6"][:]), reads=["pb6"], writes=[N["mt"]],
      name=f"p1a_m{tc}")
    def varfn(vE):
        BAR(vE, "dve", vE.tensor_tensor(S["rt"][:], S["mt"][:], S["mt"][:], ALU.mult))
        return vE.tensor_tensor(S["rt"][:], t["pb7"][:], S["rt"][:], ALU.subtract)
    A("dve", varfn, reads=[N["mt"], "pb7", N["rt"]], writes=[N["rt"]], name=f"p1a_var{tc}")
    def rfn(sc):
        BAR(sc, "act", sc.activation(S["st"][:], S["rt"][:], AF.Ln, bias=t["epsc"][:, 0:1]))
        return sc.activation(S["rt"][:], S["st"][:], AF.Exp, scale=-0.5)
    A("act", rfn, reads=[N["rt"], "epsc0", N["st"]], writes=[N["rt"], N["st"]], name=f"p1a_r{tc}")
    def ufn(vE, xc=xc):
        ins = None
        for k in range(8):
            ins = vE.tensor_tensor(t["bfB"][:, k, :], xc[:, k, :], S["mt"][:], ALU.subtract)
        BAR(vE, "dve", ins)
        for k in range(8):
            ins = vE.tensor_tensor(t["bfA"][:, k, :], t["bfB"][:, k, :], S["rt"][:], ALU.mult)
        return ins
    A("dve", ufn, reads=[xcb, N["mt"], N["rt"]], writes=["bfA", "bfB"], name=f"p1a_u{tc}")
    op = st_sdma(sch, ["bfA"], r128(scr["u1"])[:, :, 512*tc:512*tc+512], t["bfA"][:], name=f"p1a_sp{tc}")
    sch.last_writer[f"hu1_{tc}"] = op


def emit_p1_post(sch, t, v, dram, scr, tc):
    A = sch.add
    N, S = sset(t, tc % 2)
    xc = t["xcA"] if tc % 2 == 0 else t["xcB"]
    xcb = f"xc{tc % 2}"
    sdma(sch, xcb, xc[:], r128(dram["x0"])[:, :, 512*tc:512*tc+512], name=f"p1b_ldx{tc}")
    sdma(sch, "bfB", t["bfB"][:], r128(scr["u1"])[:, :, 512*tc:512*tc+512], name=f"p1b_ldu{tc}")
    sch.ops[-1].deps = sorted(set(sch.ops[-1].deps) | {sch.last_writer[f"hu1_{tc}"]})
    for phase, mkey in enumerate(("m_hi", "m_lo")):
        for pos, oc in enumerate((4, 5, 6, 7, 0, 1, 2, 3)):
            sl = (8*phase + pos) % 3
            mb = [f"pql{sl}a", f"pql{sl}b"]
            op = sdma(sch, mb, t["pql"][:, sl, :, :],
                 r128(scr[mkey])[:, :, 128*oc:128*oc+128], name=f"p1b_ldm{tc}_{phase}_{oc}")
            hp = "hi" if phase == 0 else "lo"
            sch.ops[op].deps = sorted(set(sch.ops[op].deps)
                | {sch.last_writer[f"hbm_m_{hp}_{r}_{h}"] for r in range(8) for h in range(2)})
            pb = f"pb{oc}"
            def ffn_mm(te, oc=oc, pbn=pb, phase=phase, sl=sl):
                ins = None
                for k in range(8):
                    ins = te.matmul(t[pbn][:], t["pql"][:, sl, k, :], t["bfB"][:, k, :],
                                    start=(phase == 0 and k == 0), stop=False)
                if phase == 1:
                    ins = te.matmul(t[pbn][:], t["fbrow"][0:1, 128*oc:128*oc+128],
                                    t["onesrow"][:], start=False, stop=True)
                return ins
            A("pe", ffn_mm, reads=["bfB", "fbrow", "onesrow"] + mb + ([pb] if phase else []),
              writes=[pb], name=f"p1b_ffn{tc}_{phase}_{oc}")
    for oc in (4, 5, 6, 7, 0, 1, 2, 3):
        pb = f"pb{oc}"
        A("dve", lambda vE, oc=oc, pbn=pb, xc=xc:
            vE.scalar_tensor_tensor(xc[:, oc, :], t[pbn][:], 0.5, xc[:, oc, :], ALU.mult, ALU.add),
          reads=[pb, xcb], writes=[xcb], name=f"p1b_x1_{tc}_{oc}")
    A("dve", lambda vE, xc=xc: vE.tensor_copy(t["bfA"][:], xc[:]),
      reads=[xcb], writes=["bfA"], name=f"p1b_cast{tc}")
    A("act", lambda sc, xc=xc: sc.activation(t["bfB"][:], xc[:], AF.Square),
      reads=[xcb, "bfB"], writes=["bfB"], name=f"p1b_sq{tc}")
    def stats1(te):
        ins = None
        for k in range(8):
            ins = te.matmul(t["pb6"][:], t["onesbf"][:], t["bfA"][:, k, :],
                            start=(k == 0), stop=(k == 7))
        return ins
    A("pe", stats1, reads=["bfA", "onesbf"], writes=["pb6"], name=f"p1b_s1_{tc}")
    def stats2(te):
        ins = None
        for k in range(8):
            ins = te.matmul(t["pb7"][:], t["onesbf"][:], t["bfB"][:, k, :],
                            start=(k == 0), stop=(k == 7))
        return ins
    A("pe", stats2, reads=["bfB", "onesbf"], writes=["pb7"], name=f"p1b_s2_{tc}")
    A("act", lambda sc: sc.copy(S["mt"][:], t["pb6"][:]), reads=["pb6"], writes=[N["mt"]],
      name=f"p1b_m{tc}")
    def varfn(vE):
        BAR(vE, "dve", vE.tensor_tensor(S["rt"][:], S["mt"][:], S["mt"][:], ALU.mult))
        return vE.tensor_tensor(S["fs"][:], t["pb7"][:], S["rt"][:], ALU.subtract)
    A("dve", varfn, reads=[N["mt"], "pb7", N["rt"], N["fs"]], writes=[N["fs"], N["rt"]],
      name=f"p1b_var{tc}")
    def rsfn(sc):
        BAR(sc, "act", sc.activation(S["st"][:], S["fs"][:], AF.Ln, bias=t["epsc"][:, 0:1]))
        BAR(sc, "act", sc.activation(S["rt"][:], S["st"][:], AF.Exp, scale=-0.5))
        BAR(sc, "act", sc.activation(S["st"][:], S["fs"][:], AF.Ln, bias=t["epsc"][:, 1:2], scale=1.0+EPS))
        return sc.activation(S["st"][:], S["st"][:], AF.Exp, scale=-0.5)
    A("act", rsfn, reads=[N["fs"], "epsc0", "epsc1", N["st"], N["rt"]], writes=[N["rt"], N["st"]],
      name=f"p1b_rs{tc}")
    def vfnA(vE, xc=xc):
        ins = None
        for k in range(8):
            BAR(vE, "dve", vE.tensor_tensor(S["fs"][:], xc[:, k, :], S["mt"][:], ALU.subtract))
            BAR(vE, "dve", vE.tensor_tensor(t["bfA"][:, k, :], S["fs"][:], S["st"][:], ALU.mult))
            ins = vE.tensor_scalar(t["bfA"][:, k, :], t["bfA"][:, k, :], t["c_n1w"][:, k:k+1],
                                   t["c_n1b"][:, k:k+1], ALU.mult, ALU.add)
        return ins
    A("dve", vfnA, reads=[xcb, N["mt"], N["st"], "c_n1w", "c_n1b", N["fs"]],
      writes=["bfA", N["fs"]], name=f"p1b_vA{tc}")
    def vfnC(vE):
        BAR(vE, "dve", vE.tensor_tensor(S["st"][:], S["mt"][:], S["rt"][:], ALU.mult))
        return vE.tensor_scalar_add(S["rt"][:], S["rt"][:], 1.0)
    A("dve", vfnC, reads=[N["mt"], N["rt"], N["st"]], writes=[N["rt"], N["st"]],
      name=f"p1b_vC{tc}")
    def vfnB(vE, xc=xc):
        ins = None
        for k in range(8):
            ins = vE.tensor_tensor(xc[:, k, :], xc[:, k, :], S["rt"][:], ALU.mult)
        BAR(vE, "pool", ins)
        for k in range(8):
            ins = vE.tensor_tensor(xc[:, k, :], xc[:, k, :], S["st"][:], ALU.subtract)
        return ins
    A("pool", vfnB, reads=[xcb, N["rt"], N["st"], "bfA"], writes=[xcb],
      name=f"p1b_vB{tc}")
    op = st_sdma(sch, [xcb], r128(scr["x1v"])[:, :, 512*tc:512*tc+512], xc[:], name=f"p1b_spx{tc}")
    sch.last_writer[f"hx1v_{tc}"] = op
    op = st_sdma(sch, ["bfA"], r128(scr["xln"])[:, :, 512*tc:512*tc+512], t["bfA"][:], name=f"p1b_spl{tc}")
    sch.last_writer[f"hxln_{tc}"] = op
    tkeys = ("mt", "rt", "st", "fs")
    par = tc % 2
    m4 = t["n2sc"][:, 8*par:8*par+4]
    q4 = t["n2sc"][:, 8*par+4:8*par+8]
    for tt in range(4):
        pb = f"pb{tt}"
        def xd_mm(te, tt=tt, pbn=pb):
            ins = None
            for k in range(8):
                ins = te.matmul(t[pbn][:], t["bfA"][:, k, 128*tt:128*tt+128], t["c_wdin"][:, k, :],
                                start=(k == 0), stop=(k == 7))
            return ins
        A("pe", xd_mm, reads=["bfA", "c_wdin"], writes=[pb], name=f"p1b_xd{tc}_{tt}")
    for tt in range(4):
        A("act", lambda sc, tt=tt: sc.copy(S[tkeys[tt]][:], t[f"pb{tt}"][:]),
          reads=[f"pb{tt}"], writes=[N[tkeys[tt]]], name=f"p1b_xde{tc}_{tt}")
    def n2mfn(vE):
        ins = None
        for tt in range(4):
            ins = vE.reduce_sum(m4[:, tt:tt+1], S[tkeys[tt]][:], axis=AX.X)
        BAR(vE, "dve", ins)
        return vE.tensor_scalar_mul(m4, m4, 1.0/512.0)
    A("dve", n2mfn, reads=[N[k] for k in tkeys], writes=[f"n2m{par}"], name=f"p1b_n2m{tc}")
    def n2cfn(vE):
        ins = None
        for tt in range(4):
            ins = vE.tensor_scalar(S[tkeys[tt]][:], S[tkeys[tt]][:], m4[:, tt:tt+1],
                                   None, ALU.subtract)
        return ins
    A("dve", n2cfn, reads=[f"n2m{par}"] + [N[k] for k in tkeys],
      writes=[N[k] for k in tkeys], name=f"p1b_n2c{tc}")
    for half in range(2):
        def n2sqfn(vE, half=half):
            ins = None
            for j in range(2):
                tt = 2*half + j
                ins = vE.tensor_tensor(t["pqs"][:, j, :], S[tkeys[tt]][:], S[tkeys[tt]][:],
                                       ALU.mult)
            return ins
        A("pool", n2sqfn, reads=[N[k] for k in tkeys] + ["pqs0", "pqs1"],
          writes=["pqs0", "pqs1"], name=f"p1b_n2sq{tc}_{half}")
        def n2qfn(vE, half=half):
            ins = None
            for j in range(2):
                tt = 2*half + j
                ins = vE.reduce_sum(q4[:, tt:tt+1], t["pqs"][:, j, :], axis=AX.X)
            return ins
        A("dve", n2qfn, reads=["pqs0", "pqs1"], writes=[f"n2q{par}", "pqs0", "pqs1"],
          name=f"p1b_n2q{tc}_{half}")
    def n2qsc(vE):
        return vE.tensor_scalar_mul(q4, q4, 1.0/512.0)
    A("dve", n2qsc, reads=[f"n2q{par}"], writes=[f"n2q{par}"], name=f"p1b_n2qs{tc}")
    def n2rfn(sc):
        BAR(sc, "act", sc.activation(q4, q4, AF.Ln, bias=t["epsc"][:, 0:1]))
        return sc.activation(q4, q4, AF.Exp, scale=-0.5)
    A("act", n2rfn, reads=[f"n2q{par}", "epsc0"], writes=[f"n2q{par}"], name=f"p1b_n2r{tc}")
    def xnfn(vE):
        ins = None
        for tt in range(4):
            ins = vE.tensor_scalar(S[tkeys[tt]][:], S[tkeys[tt]][:], q4[:, tt:tt+1],
                                   None, ALU.mult)
        BAR(vE, "dve", ins)
        for tt in range(4):
            ins = vE.tensor_tensor(S[tkeys[tt]][:], S[tkeys[tt]][:], t["n2wt"][:], ALU.mult)
        BAR(vE, "dve", ins)
        for tt in range(4):
            ins = vE.tensor_tensor(v["xn"][:, 4*tc+tt, :], S[tkeys[tt]][:], t["n2bt"][:], ALU.add)
        return ins
    A("dve", xnfn, reads=[f"n2q{par}", "n2wt", "n2bt"] + [N[k] for k in tkeys],
      writes=[f"xn{4*tc+tt}" for tt in range(4)] + ["H1"] + [N[k] for k in tkeys],
      name=f"p1b_xn{tc}")


def emit_p2_seq(sch, t, v, sq):
    A = sch.add
    A("dve", lambda vE: vE.memset(t["xcB"][:], 0.0), writes=["xc1", "str_", "sti"],
      name=f"p2_rst{sq}")
    A("dve", lambda vE: vE.memset(t["bfB"][:], 0.0), writes=["bfB", "strb", "nstib"],
      name=f"p2_rstb{sq}")
    for i in range(NCH):
        N, S = sset(t, i % 2)
        dtmv = [S["mt"], S["rt"]]
        dtmn = [N["mt"], N["rt"]]
        slots = (16*sq + 2*i, 16*sq + 2*i + 1)
        xnslots = [f"xn{s}" for s in slots]
        if i % 2 == 0:
            yfr_s = [t["xfr"][:, fc, :] for fc in range(3)]
            yfrn = [f"xfr{fc}" for fc in range(3)]
            yfi_s = [t["yfi"][:, fc, :] for fc in range(3)]
            yfin = [f"yfi{fc}" for fc in range(3)]
            cpl = []
        else:
            yfr_s = [t["bfA"][:, fc, :] for fc in range(3)]
            yfrn = [f"xfB{fc}" for fc in range(3)]
            yfi_s = [t["bfA"][:, 3 + fc, :] for fc in range(3)]
            yfin = [f"yfB{fc}" for fc in range(3)]
            cpl = ["bfA"]
        for n in range(4):
            for im in range(2):
                mat = "etr" if im == 0 else "eti"
                pb = f"pb{n*2+im}"
                def emm(te, n=n, mat=mat, pbn=pb, slots=slots):
                    ins = None
                    for mch in range(2):
                        ins = te.matmul(t[pbn][:], t[mat][:, mch, n, :],
                                        v["xn"][:, slots[mch], :], start=(mch == 0), stop=(mch == 1))
                    return ins
                A("pe", emm, reads=xnslots + [f"{mat}{n}0", f"{mat}{n}1"], writes=[pb],
                  name=f"p2e{sq}_{i}_{n}{im}")
                phs = v["phR"] if im == 0 else v["phI"]
                cplw = "wxg0" if im == 0 else "wxg1"
                A("act", lambda sc, n=n, pbn=pb, phs=phs: sc.copy(phs[:, 512*n:512*n+512], t[pbn][:]),
                  reads=[pb], writes=[f"ph{im}{n}", cplw], name=f"p2pe{sq}_{i}_{n}{im}")
        for n in range(4):
            eng = "dve" if n % 2 == 0 else "pool"
            pbr, pbi = f"pb{2*n}", f"pb{2*n+1}"
            sa = S["fs"] if eng == "dve" else S["st"]
            na = N["fs"] if eng == "dve" else N["st"]
            sb = S["mt"] if eng == "dve" else S["rt"]
            nb = N["mt"] if eng == "dve" else N["rt"]
            phr = v["phR"][:, 512*n:512*n+512]
            phi = v["phI"][:, 512*n:512*n+512]
            if eng == "dve":
                def foldR(vE, n=n, sa=sa, pbr=pbr, pbi=pbi):
                    vE.tensor_tensor(v["psir"][:, n, :], t["cctr"][:, n, :], t[pbr][:], ALU.mult)
                    BAR(vE, "dve", vE.tensor_tensor(sa[:], t["nccti"][:, n, :], t[pbi][:], ALU.mult))
                    return vE.tensor_tensor(v["psir"][:, n, :], v["psir"][:, n, :], sa[:], ALU.add)
                A("dve", foldR, reads=[pbr, pbi, f"cct{n}", na],
                  writes=[f"psir{n}", na, "xc0"], name=f"p2dR{sq}_{i}_{n}")
                def foldI(vE, n=n, sb=sb, phr=phr, phi=phi):
                    vE.tensor_tensor(v["psii"][:, n, :], t["cctr"][:, n, :], phi, ALU.mult)
                    BAR(vE, "pool", vE.tensor_tensor(sb[:], t["nccti"][:, n, :], phr, ALU.mult))
                    return vE.tensor_tensor(v["psii"][:, n, :], v["psii"][:, n, :], sb[:], ALU.subtract)
                A("pool", foldI,
                  reads=[f"ph0{n}", f"ph1{n}", f"cct{n}", nb, "wxg0", "wxg1"],
                  writes=[f"psii{n}", nb, "xc0"], name=f"p2dI{sq}_{i}_{n}")
            else:
                def foldR(vE, n=n, sa=sa, phr=phr, phi=phi):
                    vE.tensor_tensor(v["psir"][:, n, :], t["cctr"][:, n, :], phr, ALU.mult)
                    BAR(vE, "pool", vE.tensor_tensor(sa[:], t["nccti"][:, n, :], phi, ALU.mult))
                    return vE.tensor_tensor(v["psir"][:, n, :], v["psir"][:, n, :], sa[:], ALU.add)
                A("pool", foldR, reads=[f"ph0{n}", f"ph1{n}", f"cct{n}", na, "wxg0", "wxg1"],
                  writes=[f"psir{n}", na, "xc0"], name=f"p2dR{sq}_{i}_{n}")
                def foldI(vE, n=n, sb=sb, pbr=pbr, pbi=pbi):
                    vE.tensor_tensor(v["psii"][:, n, :], t["cctr"][:, n, :], t[pbi][:], ALU.mult)
                    BAR(vE, "dve", vE.tensor_tensor(sb[:], t["nccti"][:, n, :], t[pbr][:], ALU.mult))
                    return vE.tensor_tensor(v["psii"][:, n, :], v["psii"][:, n, :], sb[:], ALU.subtract)
                A("dve", foldI, reads=[pbr, pbi, f"cct{n}", nb],
                  writes=[f"psii{n}", nb, "xc0"], name=f"p2dI{sq}_{i}_{n}")
        for fc in range(3):
            for im in range(2):
                mat = "c_fwdc" if im == 0 else "c_fwds"
                pb = f"pb{fc*2+im}"
                def fmm(te, fc=fc, mat=mat, pbn=pb, slots=slots):
                    ins = None
                    for mch in range(2):
                        ins = te.matmul(t[pbn][:], t[mat][:, mch, 128*fc:128*fc+128],
                                        v["xn"][:, slots[mch], :], start=(mch == 0), stop=(mch == 1))
                    return ins
                A("pe", fmm, reads=xnslots + [mat], writes=[pb], name=f"p2f{sq}_{i}_{fc}{im}")
        for j, pbn in enumerate(("pb2", "pb3")):
            A("act", lambda sc, j=j, pbn=pbn: sc.copy(v["xf1"][:, 512*j:512*j+512], t[pbn][:]),
              reads=[pbn], writes=[f"xf1{j}", "wgt0"], name=f"p2xe{sq}_{i}_{j}")
        for fc in range(3):
            eng = "dve" if fc % 2 == 0 else "pool"
            pbr, pbi = f"pb{2*fc}", f"pb{2*fc+1}"
            stt = S["st"] if eng == "dve" else S["fs"]
            stn = N["st"] if eng == "dve" else N["fs"]
            if eng == "dve":
                xr, xi = t[pbr][:], t[pbi][:]
                xrd = [pbr, pbi]
            else:
                xr, xi = v["xf1"][:, 0:512], v["xf1"][:, 512:1024]
                xrd = ["xf10", "xf11", "wgt0"]
            def prodI(vE, fc=fc, eng=eng, stt=stt, xr=xr, xi=xi, yfi_s=yfi_s):
                vE.tensor_tensor(yfi_s[fc], xr, t["kfci"][:, fc, :], ALU.mult)
                BAR(vE, eng, vE.tensor_tensor(stt[:], xi, t["kfcr"][:, fc, :], ALU.mult))
                return vE.tensor_tensor(yfi_s[fc], yfi_s[fc], stt[:], ALU.add)
            A(eng, prodI, reads=xrd + [f"kfcr{fc}", f"kfci{fc}", stn],
              writes=[yfin[fc], stn] + cpl, name=f"p2pI{sq}_{i}_{fc}")
            def prodR(vE, fc=fc, eng=eng, stt=stt, xr=xr, xi=xi, yfr_s=yfr_s):
                vE.tensor_tensor(yfr_s[fc], xr, t["kfcr"][:, fc, :], ALU.mult)
                BAR(vE, eng, vE.tensor_tensor(stt[:], xi, t["kfci"][:, fc, :], ALU.mult))
                return vE.tensor_tensor(yfr_s[fc], yfr_s[fc], stt[:], ALU.subtract)
            A(eng, prodR, reads=xrd + [f"kfcr{fc}", f"kfci{fc}", stn],
              writes=[yfrn[fc], stn] + cpl, name=f"p2pR{sq}_{i}_{fc}")
        for lt in range(2):
            pb = f"pb{6+lt}"
            def ymm2(te, lt=lt, pbn=pb, first=(i == 0), yfr_s=yfr_s, yfi_s=yfi_s):
                ins = None
                seqm = []
                for fc in range(3):
                    seqm.append(("c_invc", fc, yfr_s[fc]))
                    seqm.append(("c_invs", fc, yfi_s[fc]))
                if not first:
                    for n in range(4):
                        seqm.append(("srb", n, None))
                        seqm.append(("sib", n, None))
                for ix, (mat, c, rh) in enumerate(seqm):
                    if mat in ("c_invc", "c_invs"):
                        lhs = t[mat][:, c, 128*lt:128*lt+128]
                        rhs = rh
                    else:
                        lhs = t[mat][:, c, 1+128*lt:129+128*lt]
                        rhs = v["strb"][:, c, :] if mat == "srb" else v["nstib"][:, c, :]
                    ins = te.matmul(t[pbn][:], lhs, rhs, start=(ix == 0), stop=(ix == len(seqm)-1))
                return ins
            rdy = yfrn + yfin + cpl
            if i > 0:
                rdy += ["strb", "nstib"] + [f"srb{n}" for n in range(4)] + [f"sib{n}" for n in range(4)]
            A("pe", ymm2, reads=rdy + ["c_invc", "c_invs"], writes=[pb], name=f"p2y{sq}_{i}_{lt}")
            A("act", lambda sc, lt=lt, pbn=pb, dtmv=dtmv: sc.copy(dtmv[lt][:], t[pbn][:]),
              reads=[pb], writes=[dtmn[lt]],
              name=f"p2ye{sq}_{i}_{lt}")
        def scanf(vE):
            ins = None
            for n in range(4):
                ins = vE.scalar_tensor_tensor(v["psir"][:, n, :], v["str_"][:, n, :],
                        t["lamc"][:, n:n+1], v["psir"][:, n, :], ALU.mult, ALU.add)
            for n in range(4):
                ins = vE.scalar_tensor_tensor(v["psii"][:, n, :], v["sti"][:, n, :],
                        t["lamc"][:, n:n+1], v["psii"][:, n, :], ALU.mult, ALU.add)
            BAR(vE, "dve", ins)
            for n in range(4):
                ins = vE.scalar_tensor_tensor(v["psii"][:, n, :], v["str_"][:, n, :],
                        t["lamc"][:, 4+n:5+n], v["psii"][:, n, :], ALU.mult, ALU.add)
            for n in range(4):
                ins = vE.scalar_tensor_tensor(v["str_"][:, n, :], v["sti"][:, n, :],
                        t["lamc"][:, 8+n:9+n], v["psir"][:, n, :], ALU.mult, ALU.add)
            BAR(vE, "dve", ins)
            for n in range(4):
                ins = vE.tensor_copy(v["sti"][:, n, :], v["psii"][:, n, :])
            return ins
        A("dve", scanf, reads=[f"psir{n}" for n in range(4)] + [f"psii{n}" for n in range(4)]
          + [f"lamc{n}" for n in range(4)] + ["str_", "sti", "xc0", "xc1"],
          writes=["str_", "sti", "xc0", "xc1"] + [f"psir{n}" for n in range(4)]
          + [f"psii{n}" for n in range(4)],
          name=f"p2s{sq}_{i}")
        def mirf(vE):
            vE.tensor_copy(v["strb"][:], v["str_"][:])
            return vE.tensor_scalar_mul(v["nstib"][:], v["sti"][:], -1.0)
        A("dve", mirf, reads=["str_", "sti", "xc1", "bfB"], writes=["strb", "nstib", "bfB"],
          name=f"p2mir{sq}_{i}")
        for lt in range(2):
            for q in range(4):
                pbt = "pb4" if q % 2 == 0 else "pb5"
                def trf(te, lt=lt, q=q, dtmv=dtmv, pbt=pbt):
                    return te.transpose(t[pbt][:, 0:128], dtmv[lt][:, 128*q:128*q+128],
                                        t["c_identf"][:])
                A("pe", trf, reads=[dtmn[lt], "c_identf"], writes=[pbt],
                  name=f"p2t{sq}_{i}_{lt}{q}")
                gt = 2048*sq + 256*i + 128*lt
                def evf(sc, q=q, gt=gt, pbt=pbt):
                    return sc.copy(v["dfm"][:, q, gt:gt+128], t[pbt][:, 0:128])
                A("act", evf, reads=[pbt], writes=[f"dfm{q}_{gt//512}", "H2"],
                  name=f"p2te{sq}_{i}_{lt}{q}")


def emit_p3_chunk(sch, t, v, dram, scr, tc):
    A = sch.add
    xc = t["xcA"] if tc % 2 == 0 else t["xcB"]
    xcb = f"xc{tc % 2}"
    sdma(sch, ["bfB", "strb", "nstib"], t["bfB"][:],
         r128(scr["xln"])[:, :, 512*tc:512*tc+512], name=f"p3_ldl{tc}")
    sch.ops[-1].deps = sorted(set(sch.ops[-1].deps) | {sch.last_writer[f"hxln_{tc}"]})
    gdma(sch, [xcb], xc[:], r128(scr["x1v"])[:, :, 512*tc:512*tc+512], name=f"p3_ldx{tc}")
    sch.ops[-1].deps = sorted(set(sch.ops[-1].deps) | {sch.last_writer[f"hx1v_{tc}"]})
    dfm_bufs = [f"dfm{q}_{tc}" for q in range(4)]
    for s in range(16):
        wxg = t["wxgA"] if s % 2 == 0 else t["wxgB"]
        wxgb = f"wxg{s % 2}"
        op = sdma(sch, wxgb, wxg[:], r128(scr["wxgb"])[:, :, 256*s:256*s+256], name=f"p3_lwxg{tc}_{s}")
        sch.ops[op].deps = sorted(set(sch.ops[op].deps) | {sch.last_writer["cw_wxg"]})
        wgt = t["wgtA"] if s % 2 == 0 else t["wgtB"]
        wgtb = f"wgt{s % 2}"
        op = sdma(sch, wgtb, wgt[:], r128(scr["wgtb"])[:, :, 256*s:256*s+256], name=f"p3_lwgt{tc}_{s}")
        sch.ops[op].deps = sorted(set(sch.ops[op].deps) | {sch.last_writer["cw_wgate"]})
        for j in range(2):
            pbx = f"pb{j}"
            def xgmm(te, j=j, wxg=wxg, pbn=pbx):
                ins = None
                for k in range(8):
                    ins = te.matmul(t[pbn][:], wxg[:, k, 128*j:128*j+128], t["bfB"][:, k, :],
                                    start=(k == 0), stop=(k == 7))
                return ins
            A("pe", xgmm, reads=[wxgb, "bfB"], writes=[pbx], name=f"p3xg{tc}_{s}_{j}")
            slot = 2*s + j
            A("act", lambda sc, slot=slot, pbn=pbx: sc.copy(v["xgP"][:, slot, :], t[pbn][:]),
              reads=[pbx], writes=[f"xgP{slot}", "H1"], name=f"p3xge{tc}_{s}_{j}")
            pbg = f"pb{2+j}"
            def gmm(te, j=j, wgt=wgt, pbn=pbg, tc=tc):
                ins = None
                for hc in range(4):
                    ins = te.matmul(t[pbn][:], wgt[:, hc, 128*j:128*j+128],
                                    v["dfm"][:, hc, 512*tc:512*tc+512],
                                    start=(hc == 0), stop=(hc == 3))
                return ins
            A("pe", gmm, reads=[wgtb] + dfm_bufs, writes=[pbg], name=f"p3g{tc}_{s}_{j}")
            A("dve", lambda vE, slot=slot, pbn=pbg:
                vE.tensor_tensor(v["xgP"][:, slot, :], t[pbn][:], v["xgP"][:, slot, :], ALU.mult),
              reads=[pbg, f"xgP{slot}"], writes=[f"xgP{slot}"], name=f"p3P{tc}_{s}_{j}")
    for s in range(32):
        wou = t["wouA"] if s % 2 == 0 else t["wouB"]
        woub = f"wou{s % 2}"
        op = sdma(sch, [woub], wou[:],
             r128(scr["woub"])[:, s:s+1, :], name=f"p3_lwou{tc}_{s}")
        sch.ops[op].deps = sorted(set(sch.ops[op].deps) | {sch.last_writer["cw_wout"]})
        for oc in range(8):
            pb = f"pb{oc}"
            def betamm(te, s=s, oc=oc, wou=wou, pbn=pb):
                return te.matmul(t[pbn][:], wou[:, 0, 128*oc:128*oc+128],
                                 v["xgP"][:, s, :], start=(s == 0), stop=False)
            A("pe", betamm, reads=[woub, f"xgP{s}"] + ([] if s == 0 else [pb]),
              writes=[pb] if s == 0 else [], name=f"p3b{tc}_{s}_{oc}")
    for oc in range(8):
        pb = f"pb{oc}"
        def bstop(te, oc=oc, pbn=pb):
            return te.matmul(t[pbn][:], t["borow"][0:1, 128*oc:128*oc+128], t["onesrow"][:],
                             start=False, stop=True)
        A("pe", bstop, reads=["borow", "onesrow", pb], writes=[pb], name=f"p3bs{tc}_{oc}")
    for oc in range(8):
        pb = f"pb{oc}"
        A("dve", lambda vE, oc=oc, pbn=pb, xc=xc:
            vE.tensor_tensor(xc[:, oc, :], t[pbn][:], xc[:, oc, :], ALU.add),
          reads=[pb, xcb], writes=[xcb], name=f"p3x2_{tc}_{oc}")


def emit_p4_chunk(sch, t, v, dram, scr, yout, tc):
    A = sch.add
    N, S = sset(t, tc % 2)
    xc = t["xcA"] if tc % 2 == 0 else t["xcB"]
    xcb = f"xc{tc % 2}"
    A("dve", lambda vE, xc=xc: vE.tensor_copy(t["bfA"][:], xc[:]),
      reads=[xcb, "bfA"], writes=["bfA"], name=f"p4_cast{tc}")
    def stats1(te):
        ins = None
        for k in range(8):
            ins = te.matmul(t["pb6"][:], t["onesbf"][:], t["bfA"][:, k, :],
                            start=(k == 0), stop=(k == 7))
        return ins
    A("pe", stats1, reads=["bfA", "onesbf"], writes=["pb6"], name=f"p4_s1_{tc}")
    A("act", lambda sc: sc.activation(t["bfA"][:], t["bfA"][:], AF.Square),
      reads=["bfA"], writes=["bfA"], name=f"p4_sq{tc}")
    def stats2(te):
        ins = None
        for k in range(8):
            ins = te.matmul(t["pb7"][:], t["onesbf"][:], t["bfA"][:, k, :],
                            start=(k == 0), stop=(k == 7))
        return ins
    A("pe", stats2, reads=["bfA", "onesbf"], writes=["pb7"], name=f"p4_s2_{tc}")
    A("act", lambda sc: sc.copy(S["mt"][:], t["pb6"][:]), reads=["pb6"],
      writes=[N["mt"]], name=f"p4_m{tc}")
    def varfn(vE):
        BAR(vE, "dve", vE.tensor_tensor(S["rt"][:], S["mt"][:], S["mt"][:], ALU.mult))
        return vE.tensor_tensor(S["rt"][:], t["pb7"][:], S["rt"][:], ALU.subtract)
    A("dve", varfn, reads=[N["mt"], "pb7", N["rt"]], writes=[N["rt"]], name=f"p4_var{tc}")
    def rfn(sc):
        BAR(sc, "act", sc.activation(S["st"][:], S["rt"][:], AF.Ln, bias=t["epsc"][:, 0:1]))
        return sc.activation(S["rt"][:], S["st"][:], AF.Exp, scale=-0.5)
    A("act", rfn, reads=[N["rt"], "epsc0", N["st"]], writes=[N["rt"], N["st"]], name=f"p4_r{tc}")
    def ufn(vE, xc=xc):
        ins = None
        for k in range(8):
            ins = vE.tensor_tensor(t["bfA"][:, k, :], xc[:, k, :], S["mt"][:], ALU.subtract)
        BAR(vE, "dve", ins)
        for k in range(8):
            ins = vE.tensor_tensor(t["bfA"][:, k, :], t["bfA"][:, k, :], S["rt"][:], ALU.mult)
        return ins
    A("dve", ufn, reads=[xcb, N["mt"], N["rt"], "bfA"], writes=["bfA"], name=f"p4_u{tc}")
    for phase, mkey in enumerate(("m_hi", "m_lo")):
        for pos, oc in enumerate((4, 5, 6, 7, 0, 1, 2, 3)):
            sl = (8*phase + pos) % 3
            mb = [f"pql{sl}a", f"pql{sl}b"]
            op = sdma(sch, mb, t["pql"][:, sl, :, :],
                 r128(scr[mkey])[:, :, 128*oc:128*oc+128], name=f"p4_ldm{tc}_{phase}_{oc}")
            hp = "hi" if phase == 0 else "lo"
            sch.ops[op].deps = sorted(set(sch.ops[op].deps)
                | {sch.last_writer[f"hbm_m_{hp}_{r}_{h}"] for r in range(8) for h in range(2)})
            pb = f"pb{oc}"
            def ffn_mm(te, oc=oc, pbn=pb, phase=phase, sl=sl):
                ins = None
                for k in range(8):
                    ins = te.matmul(t[pbn][:], t["pql"][:, sl, k, :], t["bfA"][:, k, :],
                                    start=(phase == 0 and k == 0), stop=False)
                if phase == 1:
                    ins = te.matmul(t[pbn][:], t["fbrow"][0:1, 128*oc:128*oc+128],
                                    t["onesrow"][:], start=False, stop=True)
                return ins
            A("pe", ffn_mm, reads=["bfA", "fbrow", "onesrow"] + mb + ([pb] if phase else []),
              writes=[pb], name=f"p4_ffn{tc}_{phase}_{oc}")
    for oc in (4, 5, 6, 7, 0, 1, 2, 3):
        pb = f"pb{oc}"
        A("dve", lambda vE, oc=oc, pbn=pb, xc=xc:
            vE.scalar_tensor_tensor(xc[:, oc, :], t[pbn][:], 0.5, xc[:, oc, :], ALU.mult, ALU.add),
          reads=[pb, xcb], writes=[xcb], name=f"p4_x3_{tc}_{oc}")
    st_sdma(sch, [xcb], r128(yout)[:, :, 512*tc:512*tc+512], xc[:], name=f"p4_out{tc}")
    sch.last_writer[f"yout{tc}"] = sch.ops[-1].idx


def emit_debug(sch, t, v, scr, dbg, dbgset):
    if not dbgset:
        return
    for nm in ("x1v", "x1raw", "u1", "xln"):
        if nm in dbgset:
            st_sdma(sch, [], dbg[nm], scr[nm], name=f"dbg_{nm}")
    for nm, key in (("pthi", "pt_hi"), ("ptlo", "pt_lo"), ("qthi", "qt_hi"),
                    ("wrhi", "wrhi"), ("wrlo", "wrlo"), ("mlo", "m_lo")):
        if nm in dbgset:
            st_sdma(sch, [], dbg[nm], scr[key], name=f"dbg_{nm}")
    if "mmat" in dbgset:
        op = st_sdma(sch, [], dbg["mmat"], scr["m_hi"], name="dbg_mmat")
        sch.ops[op].deps = sorted(set(sch.ops[op].deps)
                                  | {sch.last_writer[f"hbm_m_hi_{r}_{h}"] for r in range(8) for h in range(2)})
    if "xn" in dbgset:
        st_sdma(sch, [f"xn{s}" for s in range(32)],
                dbg["xn"].rearrange("(a p) h -> p a h", p=128), v["xn"], name="dbg_xn")
    if "kp" in dbgset:
        st_sdma(sch, ["kp0", "kp1", "pqs0", "pqs1"], dbg["kp"].rearrange("(a p) h -> p a h", p=128),
                t["kp"][:], name="dbg_kp")
    if "kfcr" in dbgset:
        st_sdma(sch, [f"kfcr{c}" for c in range(3)],
                dbg["kfcr"].rearrange("(a p) h -> p a h", p=128), t["kfcr"][:], name="dbg_kfcr")
    if "kfci" in dbgset:
        st_sdma(sch, [f"kfci{c}" for c in range(3)],
                dbg["kfci"].rearrange("(a p) h -> p a h", p=128), t["kfci"][:], name="dbg_kfci")
    if "srb" in dbgset:
        st_sdma(sch, [f"srb{n}" for n in range(4)],
                dbg["srb"].rearrange("(a p) h -> p a h", p=128), t["srb"][:], name="dbg_srb")
    if "sib" in dbgset:
        st_sdma(sch, [f"sib{n}" for n in range(4)],
                dbg["sib"].rearrange("(a p) h -> p a h", p=128), t["sib"][:], name="dbg_sib")
    if "cctr" in dbgset:
        st_sdma(sch, [f"cct{n}" for n in range(4)],
                dbg["cctr"].rearrange("(a p) h -> p a h", p=128), t["cctr"][:], name="dbg_cctr")
    if "nccti" in dbgset:
        st_sdma(sch, [f"cct{n}" for n in range(4)],
                dbg["nccti"].rearrange("(a p) h -> p a h", p=128), t["nccti"][:], name="dbg_nccti")
    if "dfm" in dbgset:
        st_sdma(sch, [f"dfm{q}_{c}" for q in range(4) for c in range(8)],
                dbg["dfm"].rearrange("(a p) tt -> p a tt", p=128), v["dfm"], name="dbg_dfm")


def make_inputs(inputs, core):
    cst = host_constants()
    idx = inputs["L0_attn_idx"]
    x0 = np.ascontiguousarray(
        idx[BPC*core:BPC*core+BPC].reshape(BPC*SEQ, D_MODEL).T)
    m = {
        "x0": x0,
        "wxg": inputs["L0_attn_Wxg"],
        "wdin": inputs["L0_attn_Wdin"],
        "wgate": inputs["L0_attn_Wgate"],
        "wout": inputs["L0_attn_Wout"],
        "wfr": inputs["L0_ffn_w_freq_real"],
        "wfi": inputs["L0_ffn_w_freq_imag"],
        "crt": np.ascontiguousarray(inputs["L0_attn_CR"].T),
        "cit": np.ascontiguousarray(inputs["L0_attn_CI"].T),
        "lr": np.ascontiguousarray(inputs["L0_attn_LR"].reshape(4, 128).T),
        "li": np.ascontiguousarray(inputs["L0_attn_LI"].reshape(4, 128).T),
        "dvec": inputs["L0_attn_D"].reshape(1, 512),
        "n1w": np.ascontiguousarray(inputs["L0_attn_n1_w"].reshape(8, 128).T),
        "n1b": np.ascontiguousarray(inputs["L0_attn_n1_b"].reshape(8, 128).T),
        "n2w": inputs["L0_attn_n2_w"].reshape(1, 512),
        "n2b": inputs["L0_attn_n2_b"].reshape(1, 512),
        "fbias": inputs["L0_ffn_bias"].reshape(1, D_MODEL),
        "bout": inputs["L0_attn_bout"].reshape(1, D_MODEL),
        "cmat": cst["cmat"], "smat": cst["smat"],
        "fwdc": cst["fwdc"], "fwds": cst["fwds"],
        "invc": cst["invc"], "invs": cst["invs"],
        "identf": cst["identf"],
        "jrow": cst["jrow"],
    }
    return {k: np.ascontiguousarray(val) for k, val in m.items()}


def kernel(**inputs):
    nc = build_program()
    in_maps = [make_inputs(inputs, c) for c in range(NCORES)]
    res = run_bass_kernel_spmd(nc, in_maps, core_ids=list(range(NCORES)))
    outs = []
    for c in range(NCORES):
        y = np.asarray(res.results[c]["y"])
        outs.append(np.ascontiguousarray(y.T).reshape(BPC, SEQ, D_MODEL))
    return np.concatenate(outs, 0)
